# version 06
# baseline (speedup 1.0000x reference)
import numpy as np
import ml_dtypes
import concourse.bass as bass
import concourse.mybir as mybir
from concourse.bass_utils import run_bass_kernel_spmd
from contextlib import ExitStack

F32 = mybir.dt.float32
BF16 = mybir.dt.bfloat16
AF = mybir.ActivationFunctionType
ALU = mybir.AluOpType
AX = mybir.AxisListType

NCORE = 8
DIM = 1024
NDHI = 8
TOK = 2048
HALF = 1024
NQ = 2
QW = 512
STATE = 64
CHUNK = 64
NE = 64
NEA = 68
I_MOE = 512
EPS_LN = 1e-5
EPS_RMS = 1e-6
BIGNEG = -1e30
NQUAD = 17
CONST_VALS = [1e-5, 1e-6, 1.0, -0.5, -1.0, 1.0 / 1024.0, 0.5, 2.0, -1e30, 64.0,
              1e-20, 1.25, 0.0, -1.0 / 1024.0, 8.0, 4.0]


from contextlib import contextmanager


@contextmanager
def fori(nc, start, end, step=1):
    import concourse.mybir as _mb
    engines = _mb.ALL_ENGINES
    loop_id = nc.next_id()
    loop_name = f"kfori_{loop_id}"
    loop_start = f"{loop_name}_loop"
    loop_end = f"{loop_name}_end"
    registers = nc.alloc_registers(f"{loop_name}_i", engines=engines)
    nc.regs_mov(registers, start)
    nc.br(loop_start, engines=engines)
    snapped = [None]
    with nc.body(loop_start, valid_engines=engines):
        sv = nc.snap(registers, min_val=start, max_val=end - 1)
        snapped[0] = sv
        yield sv
        nc.regs_alu(registers, registers, step, op=mybir.AluOpType.add)
        nc.br_lt(registers, end, on_true=loop_start, on_false=loop_end, engines=engines)
    nc.switch_bb(loop_end)
    for handles in (registers, getattr(snapped[0], "val", None)):
        try:
            hs = handles.handles if hasattr(handles, "handles") else None
            if hs:
                for h in hs:
                    nc.engines[h.engine].free_register(h)
        except Exception:
            pass


class Ctr:
    def __init__(self):
        self.v = 0

    def inc(self, n=1):
        self.v += n
        return self.v


class KB:

    def __init__(self, nc):
        self.nc = nc
        self.g, self.t, self.s, self.v, self.sy = (
            nc.gpsimd, nc.tensor, nc.scalar, nc.vector, nc.sync)
        self.ctx = ExitStack()
        e_ = self.ctx.enter_context
        self.sem = {}
        self.c = {}
        for name in ["d_sy", "d_g", "d_wu", "d_wd", "d_c", "d_st", "d_out",
                     "s_pe", "s_sc", "s_ve", "s_gp"]:
            self.sem[name] = e_(nc.semaphore(name))
            self.c[name] = Ctr()

    def inc(self, eng, instr, sem, n=1):
        instr.then_inc(self.sem[sem], n)
        self.c[sem].inc(n)

    def wait(self, eng, sem):
        eng.wait_ge(self.sem[sem], self.c[sem].v)

    def init_tr(self, id_f):
        self.id_f = id_f
        nc = self.nc
        self.ptr = [self.ctx.enter_context(nc.psum_tensor(f"ptrT{i}", [128, 512], F32))
                    for i in range(2)]
        self.tr_i = 0

    def pe_t(self, out_ap, in_ap):
        t, s = self.t, self.s
        pi, fi = in_ap.shape
        pb = in_ap.base_partition()
        bank = self.ptr[self.tr_i % 2]
        self.tr_i += 1
        t.wait_ge(self.sem["s_sc"], self.c["s_sc"].v)
        self.inc("t", t.transpose(bank[0:fi, 0:pi], in_ap,
                                  self.id_f[pb:pb + pi, pb:pb + pi]), "s_pe")
        s.wait_ge(self.sem["s_pe"], self.c["s_pe"].v)
        self.inc("s", s.copy(out_ap, bank[0:fi, 0:pi]), "s_sc")

    def init_bc(self, ones_f):
        self.ones_f = ones_f
        self.pbc = [self.ctx.enter_context(self.nc.psum_tensor(f"pbcB{i}", [128, 512], F32))
                    for i in range(2)]
        self.bc_i = 0

    def bc_row(self, out_ap, src_row_ap):
        t, s = self.t, self.s
        w = src_row_ap.shape[1]
        bank = self.pbc[self.bc_i % 2]
        self.bc_i += 1
        t.wait_ge(self.sem["s_sc"], self.c["s_sc"].v)
        self.inc("t", t.matmul(bank[:, 0:w], self.ones_f[:], src_row_ap,
                               start=True, stop=True), "s_pe")
        s.wait_ge(self.sem["s_pe"], self.c["s_pe"].v)
        self.inc("s", s.copy(out_ap, bank[:, 0:w]), "s_sc")


def build(stage=5):
    nc = bass.Bass("TRN2", target_bir_lowering=False)

    x_in = nc.dram_tensor("x_in", [TOK, DIM], F32, kind="ExternalInput").ap()
    all_up = nc.dram_tensor("all_up", [NEA, DIM, I_MOE], F32, kind="ExternalInput").ap()
    all_dn = nc.dram_tensor("all_dn", [NEA, I_MOE, DIM], F32, kind="ExternalInput").ap()
    router_w = nc.dram_tensor("router_w", [NE, DIM], F32, kind="ExternalInput").ap()
    cb_in = nc.dram_tensor("cb_in", [1, NE], F32, kind="ExternalInput").ap()
    xpw_in = nc.dram_tensor("xpw_in", [192, DIM], F32, kind="ExternalInput").ap()
    dtpw_in = nc.dram_tensor("dtpw_in", [DIM, STATE], F32, kind="ExternalInput").ap()
    dtb_in = nc.dram_tensor("dtb_in", [NDHI, 128], F32, kind="ExternalInput").ap()
    alog_in = nc.dram_tensor("alog_in", [DIM, STATE], F32, kind="ExternalInput").ap()
    dvec_in = nc.dram_tensor("dvec_in", [NDHI, 128], F32, kind="ExternalInput").ap()
    nw_in = nc.dram_tensor("nw_in", [NDHI, 128], F32, kind="ExternalInput").ap()
    opw_in = nc.dram_tensor("opw_in", [DIM, DIM], F32, kind="ExternalInput").ap()
    ident_in = nc.dram_tensor("ident_in", [128, 128], F32, kind="ExternalInput").ap()
    ltg_in = nc.dram_tensor("ltg_in", [1, 64], F32, kind="ExternalInput").ap()
    consts_in = nc.dram_tensor("consts_in", [128, 16], F32, kind="ExternalInput").ap()
    lt64_in = nc.dram_tensor("lt64_in", [8, 512], F32, kind="ExternalInput").ap()

    out = nc.dram_tensor("out", [TOK, DIM], F32, kind="ExternalOutput").ap()
    c_stage = nc.dram_tensor("c_stage", [NEA, TOK], BF16, kind="Internal").ap()
    b_stage = nc.dram_tensor("b_stage", [8, STATE * CHUNK], F32, kind="Internal").ap()
    cc_stage = nc.dram_tensor("cc_stage", [8, STATE * CHUNK], F32, kind="Internal").ap()
    zd_stage = nc.dram_tensor("zd_stage", [128, 4096], BF16, kind="Internal").ap()

    dbg = dbg_c = None
    if stage < 5:
        dbg = nc.dram_tensor("dbg", [TOK, DIM], F32, kind="ExternalOutput").ap()
        dbg_c = nc.dram_tensor("dbg_c", [NEA, TOK], F32, kind="ExternalOutput").ap()

    kb = KB(nc)
    g, t, s, v, sy = kb.g, kb.t, kb.s, kb.v, kb.sy
    sem, C = kb.sem, kb.c

    with kb.ctx:
        e_ = kb.ctx.enter_context

        xs = e_(nc.sbuf_tensor("xs", [128, NDHI * TOK], F32))
        id_f = e_(nc.sbuf_tensor("id_f", [128, 128], F32))
        ones_f = e_(nc.sbuf_tensor("ones_f", [1, 128], F32))
        ones_b = e_(nc.sbuf_tensor("ones_b", [1, 128], BF16))
        onescol_f = e_(nc.sbuf_tensor("onescol_f", [128, 1], F32))
        rwT = e_(nc.sbuf_tensor("rwT", [128, NDHI * 64], F32))

        a_fm = e_(nc.sbuf_tensor("a_fm", [128, NDHI * STATE], F32))
        a_fmb = e_(nc.sbuf_tensor("a_fmb", [128, NDHI * STATE], BF16))
        dtb_fm = e_(nc.sbuf_tensor("dtb_fm", [128, NDHI], F32))
        dvec_fm = e_(nc.sbuf_tensor("dvec_fm", [128, NDHI], F32))
        nw_fm = e_(nc.sbuf_tensor("nw_fm", [128, NDHI], F32))
        cb_bc = e_(nc.sbuf_tensor("cb_bc", [128, 64], F32))
        ltg_bc = e_(nc.sbuf_tensor("ltg_bc", [128, 64], F32))

        consts_sb = e_(nc.sbuf_tensor("consts_sb", [128, 16], F32))
        for _ci, _cv in enumerate(CONST_VALS):
            nc.const_aps.register(F32, float(_cv), consts_sb[:, _ci:_ci + 1])
        scratch1 = e_(nc.sbuf_tensor("scratch1", [128, 8192], F32))
        lt64_bc = scratch1[:, 0:4096]
        scratch2 = e_(nc.sbuf_tensor("scratch2", [128, 6144], F32))
        sc_tm = scratch2[:, 0:512]
        sfc = scratch2[:, 512:1024]
        rnk = scratch2[:, 1024:1536]
        rtmp = scratch2[:, 1536:2048]
        rtmp2 = scratch2[:, 2048:2560]
        mean_bc = scratch2[:, 2560:3072]
        rstd_bc = scratch2[:, 3072:3584]
        gsml = scratch2[:, 3584:3648]
        gsml2 = scratch2[:, 3648:3712]
        gsml3 = scratch2[:, 3712:3776]
        gsml4 = scratch2[:, 3776:3840]
        stat = scratch2[0:1, 3840:5376]
        c_fmb = scratch2[:, 5376:5888].bitcast(BF16)[0:NEA, :]
        wu_s = e_(nc.sbuf_tensor("wu_s", [128, 4 * NDHI * 512], BF16))
        opwT = wu_s[:, 0:NDHI * DIM]
        wd_s = e_(nc.sbuf_tensor("wd_s", [128, 4 * 4 * DIM], BF16))
        g2_s = scratch1[:].bitcast(BF16)

        cbc_s = scratch2[:, 0:2048].bitcast(BF16)
        r_bf = scratch2[:, 2048:3072].bitcast(BF16)
        rc_bf = scratch2[:, 3072:3584].bitcast(BF16)
        crow_s = scratch2[0:1, 3584:5632].bitcast(BF16)
        tm_tile = scratch2[:, 0:1024]
        xpwT = wd_s[:, 0:1536]
        dtpwT = wd_s[0:64, 1536:2560]
        xlnb = e_(nc.sbuf_tensor("xlnb", [128, NDHI * HALF], BF16))
        xln32 = scratch1[:, 4096:8192]

        kb.init_tr(id_f)
        kb.init_bc(ones_f)

        xs3 = xs[:].rearrange("p (h k) -> p h k", h=NDHI)

        def setup():
            kb.inc("g", g.dma_start(out=consts_sb[:], in_=consts_in), "d_g", 16)
            kb.inc("g", g.dma_start(out=id_f[:], in_=ident_in), "d_g", 16)
            kb.inc("g", g.dma_start(
                out=a_fm[:].rearrange("p (h n) -> p h n", h=NDHI),
                in_=alog_in.rearrange("(h p) n -> p h n", p=128)), "d_g", 16)
            kb.inc("g", g.dma_start(out=tm_tile[0:8, 0:128], in_=dtb_in), "d_g", 16)
            kb.inc("g", g.dma_start(out=tm_tile[32:40, 0:128], in_=dvec_in), "d_g", 16)
            kb.inc("g", g.dma_start(out=tm_tile[64:72, 0:128], in_=nw_in), "d_g", 16)
            kb.inc("g", g.dma_start(out=tm_tile[0:1, 128:192], in_=cb_in), "d_g", 16)
            kb.inc("g", g.dma_start(out=tm_tile[0:1, 192:256], in_=ltg_in), "d_g", 16)
            kb.inc("g", g.dma_start(out=xs[0:1, 0:4096], in_=lt64_in.rearrange("a b -> (a b)").unsqueeze(0)), "d_g", 16)
            g.wait_ge(sem["d_g"], C["d_g"].v)
            kb.inc("g", g.memset(ones_f[:], 1.0), "s_gp")
            kb.inc("g", g.memset(ones_b[:], 1.0), "s_gp")
            kb.inc("g", g.memset(onescol_f[:], 1.0), "s_gp")
            t.wait_ge(sem["s_gp"], C["s_gp"].v)
            s.wait_ge(sem["s_gp"], C["s_gp"].v)
            v.wait_ge(sem["s_gp"], C["s_gp"].v)
            kb.inc("s", s.activation(a_fm[:], a_fm[:], AF.Exp), "s_sc")
            kb.wait(v, "s_sc")
            kb.inc("v", v.tensor_scalar_mul(a_fm[:], a_fm[:], -1.0), "s_ve")
            kb.wait(v, "s_ve")
            kb.inc("v", v.tensor_copy(a_fmb[:], a_fm[:]), "s_ve")
            kb.pe_t(dtb_fm[:], tm_tile[0:8, 0:128])
            kb.pe_t(dvec_fm[:], tm_tile[32:40, 0:128])
            kb.pe_t(nw_fm[:], tm_tile[64:72, 0:128])
            kb.bc_row(cb_bc[:], tm_tile[0:1, 128:192])
            kb.bc_row(ltg_bc[:], tm_tile[0:1, 192:256])
            for j in range(8):
                kb.bc_row(lt64_bc[:, bass.ts(j, 512)], xs[0:1, bass.ts(j, 512)])
            kb.wait(sy, "s_sc")
            kb.inc("sy", sy.dma_start(out=tm_tile[0:64, :], in_=router_w), "d_sy", 16)
            t.wait_ge(sem["d_sy"], C["d_sy"].v)
            s.wait_ge(sem["d_sy"], C["d_sy"].v)
            for dc in range(NDHI):
                kb.pe_t(rwT[:, bass.ts(dc, 64)], tm_tile[0:64, bass.ts(dc, 128)])
            nc.all_engine_barrier()

        def build_opwT():
            kb.wait(sy, "s_sc")
            kb.inc("sy", sy.dma_start(out=tm_tile, in_=xpw_in[0:128, :]), "d_sy", 16)
            t.wait_ge(sem["d_sy"], C["d_sy"].v)
            for dc in range(NDHI):
                kb.pe_t(xpwT[:, dc * 192: dc * 192 + 128], tm_tile[:, bass.ts(dc, 128)])
            kb.wait(sy, "s_sc")
            kb.inc("sy", sy.dma_start(out=tm_tile[0:64, :], in_=xpw_in[128:192, :]), "d_sy", 16)
            t.wait_ge(sem["d_sy"], C["d_sy"].v)
            for dc in range(NDHI):
                kb.pe_t(xpwT[:, dc * 192 + 128: dc * 192 + 192], tm_tile[0:64, bass.ts(dc, 128)])
            kb.wait(sy, "s_sc")
            kb.inc("sy", sy.dma_start(
                out=tm_tile[:, 0:512].rearrange("p (h r) -> p h r", h=NDHI),
                in_=dtpw_in.rearrange("(h p) r -> p h r", p=128)), "d_sy", 16)
            t.wait_ge(sem["d_sy"], C["d_sy"].v)
            for dc in range(NDHI):
                kb.pe_t(dtpwT[:, bass.ts(dc, 128)], tm_tile[:, bass.ts(dc, 64)])
            for ohi in range(NDHI):
                kb.wait(sy, "s_sc")
                kb.inc("sy", sy.dma_start(out=tm_tile, in_=opw_in[bass.ts(ohi, 128), :]), "d_sy", 16)
                t.wait_ge(sem["d_sy"], C["d_sy"].v)
                for dc in range(NDHI):
                    kb.pe_t(opwT[:, dc * DIM + ohi * 128: dc * DIM + ohi * 128 + 128],
                            tm_tile[:, bass.ts(dc, 128)])
            nc.all_engine_barrier()

        def load_x_half(half):
            for blk in range(8):
                tok0 = half * HALF + blk * 128
                kb.wait(sy, "s_sc")
                kb.inc("sy", sy.dma_start(out=tm_tile, in_=x_in[tok0:tok0 + 128, :]), "d_sy", 16)
                t.wait_ge(sem["d_sy"], C["d_sy"].v)
                for dc in range(NDHI):
                    kb.pe_t(xs[:, dc * TOK + tok0: dc * TOK + tok0 + 128],
                            tm_tile[:, bass.ts(dc, 128)])
            nc.all_engine_barrier()

        def layernorm_q(half, q, pst):
            tok0 = half * HALF + q * QW
            xq = xs3[:, :, tok0:tok0 + QW]
            x3 = xln32.rearrange("p (h k) -> p h k", h=NDHI)
            kb.wait(v, "s_sc")
            kb.wait(v, "s_pe")
            kb.inc("v", v.tensor_tensor(x3, xq, xq, ALU.mult), "s_ve")
            kb.wait(t, "s_ve")
            for dc in range(NDHI):
                mm = t.matmul(pst[0:1, 0:QW], onescol_f[:], xq[:, dc, :],
                              start=(dc == 0), stop=(dc == NDHI - 1))
            kb.inc("t", mm, "s_pe")
            for dc in range(NDHI):
                mm = t.matmul(pst[0:1, QW:2 * QW], onescol_f[:], xln32[:, bass.ts(dc, QW)],
                              start=(dc == 0), stop=(dc == NDHI - 1))
            kb.inc("t", mm, "s_pe")
            kb.wait(s, "s_pe")
            kb.wait(s, "s_ve")
            kb.inc("s", s.mul(stat[0:1, 0:QW], pst[0:1, 0:QW], 1.0 / DIM), "s_sc")
            kb.inc("s", s.mul(stat[0:1, QW:2 * QW], pst[0:1, QW:2 * QW], 1.0 / DIM), "s_sc")
            kb.wait(v, "s_sc")
            kb.inc("v", v.tensor_tensor(stat[0:1, 2 * QW:3 * QW], stat[0:1, 0:QW],
                                        stat[0:1, 0:QW], ALU.mult), "s_ve")
            kb.wait(v, "s_ve")
            kb.inc("v", v.tensor_tensor(stat[0:1, QW:2 * QW], stat[0:1, QW:2 * QW],
                                        stat[0:1, 2 * QW:3 * QW], ALU.subtract), "s_ve")
            kb.wait(s, "s_ve")
            kb.inc("s", s.activation(stat[0:1, QW:2 * QW], stat[0:1, QW:2 * QW],
                                     AF.Ln, bias=EPS_LN), "s_sc")
            kb.wait(s, "s_sc")
            kb.inc("s", s.activation(stat[0:1, QW:2 * QW], stat[0:1, QW:2 * QW],
                                     AF.Exp, scale=-0.5), "s_sc")
            kb.wait(t, "s_sc")
            kb.bc_row(mean_bc[:], stat[0:1, 0:QW])
            kb.bc_row(rstd_bc[:], stat[0:1, QW:2 * QW])
            mb = mean_bc.unsqueeze(1).broadcast_to([128, NDHI, QW])
            rb = rstd_bc.unsqueeze(1).broadcast_to([128, NDHI, QW])
            kb.wait(v, "s_sc")
            kb.inc("v", v.tensor_tensor(x3, xq, mb, ALU.subtract), "s_ve")
            kb.wait(v, "s_ve")
            kb.inc("v", v.tensor_tensor(x3, x3, rb, ALU.mult), "s_ve")
            xlb = xlnb[:].rearrange("p (h k) -> p h k", h=NDHI)[:, :, q * QW:(q + 1) * QW]
            kb.wait(v, "s_ve")
            kb.inc("v", v.tensor_copy(xlb, x3), "s_ve")

        def routing_half(half, pst, plg, tag="A"):
            tl = 8
            for q in range(NQ):
                layernorm_q(half, q, pst)
                kb.wait(t, "s_ve")
                kb.wait(t, "s_sc")
                for dc in range(NDHI):
                    mm = t.matmul(plg[0:64, 0:QW], rwT[:, bass.ts(dc, 64)],
                                  xln32[:, bass.ts(dc, QW)], start=(dc == 0), stop=(dc == NDHI - 1))
                kb.inc("t", mm, "s_pe")
                kb.wait(s, "s_pe")
                kb.inc("s", s.activation(rtmp2[0:64, 0:QW], plg[0:64, 0:QW], AF.Sigmoid), "s_sc")
                kb.wait(t, "s_sc")
                for b_ in range(4):
                    kb.pe_t(sc_tm[:, (q * 4 + b_) * 64:(q * 4 + b_) * 64 + 64],
                            rtmp2[0:64, bass.ts(b_, 128)])
            s3 = sc_tm.rearrange("p (a e) -> p a e", a=tl)
            f3 = sfc.rearrange("p (a e) -> p a e", a=tl)
            f4 = sfc.rearrange("p (a g e) -> p a g e", a=tl, g=8)
            r4 = rtmp.rearrange("p (a g e) -> p a g e", a=tl, g=8)
            r42 = rtmp2.rearrange("p (a g e) -> p a g e", a=tl, g=8)
            m3 = rtmp.rearrange("p (a e) -> p a e", a=tl)
            n3 = rnk.rearrange("p (a e) -> p a e", a=tl)
            e3 = rtmp2.rearrange("p (a e) -> p a e", a=tl)
            g1 = gsml.rearrange("p (a k) -> p a k", a=tl)
            g2_ = gsml2.rearrange("p (a k) -> p a k", a=tl)
            g3 = gsml3.rearrange("p (a k) -> p a k", a=tl)
            g4 = gsml4.rearrange("p (a k) -> p a k", a=tl)
            cbb = cb_bc[:].unsqueeze(1).broadcast_to([128, tl, 64])

            def V(op):
                kb.wait(v, "s_ve")
                kb.wait(v, "s_sc")
                kb.inc("v", op(), "s_ve")
            V(lambda: v.tensor_tensor(f3, s3, cbb, ALU.add))
            V(lambda: v.tensor_reduce(g1.unsqueeze(3), f4, AX.X, ALU.max))
            m1b = g1.unsqueeze(3).broadcast_to([128, tl, 8, 8])
            V(lambda: v.tensor_tensor(r4, f4, m1b, ALU.is_equal))
            V(lambda: v.tensor_reduce(g2_.unsqueeze(3), r4, AX.X, ALU.add))
            V(lambda: v.tensor_scalar(r42, r4, BIGNEG, 0.0, ALU.mult, ALU.add))
            V(lambda: v.tensor_tensor(r42, r42, f4, ALU.add))
            V(lambda: v.tensor_reduce(g3.unsqueeze(3), r42, AX.X, ALU.max))
            V(lambda: v.tensor_scalar(g4, g2_, 2.0, 0.0, ALU.is_ge, ALU.add))
            V(lambda: v.tensor_tensor(g2_, g1, g3, ALU.subtract))
            V(lambda: v.tensor_tensor(g2_, g2_, g4, ALU.mult))
            V(lambda: v.tensor_tensor(g3, g3, g2_, ALU.add))
            V(lambda: v.tensor_tensor(g1, g1, g3, ALU.add))
            V(lambda: v.memset(gsml4[:], 0.0))
            for j in range(8):
                gj = g1[:, :, j:j + 1].broadcast_to([128, tl, 8])
                ltj = ltg_bc[:, j * 8:(j + 1) * 8].unsqueeze(1).broadcast_to([128, tl, 8])
                V(lambda: v.tensor_tensor(g2_, gj, g1, ALU.is_gt))
                V(lambda: v.tensor_tensor(g4, g4, g2_, ALU.add))
                V(lambda: v.tensor_tensor(g2_, gj, g1, ALU.is_equal))
                V(lambda: v.tensor_tensor(g2_, g2_, ltj, ALU.mult))
                V(lambda: v.tensor_tensor(g4, g4, g2_, ALU.add))
            V(lambda: v.tensor_scalar(g4, g4, 4.0, 0.0, ALU.is_lt, ALU.add))
            gselb = g4.unsqueeze(3).broadcast_to([128, tl, 8, 8])
            V(lambda: v.tensor_tensor(r4, f4, gselb, ALU.mult))
            V(lambda: v.memset(rnk[:], 0.0))
            ve_reg = v.alloc_register(f"ve_rank{tag}{half}")
            v.reg_mov(ve_reg, C["s_ve"].v)
            with fori(nc, 0, 64) as j:
                mjv = m3[:, :, bass.ts(j, 1)].broadcast_to([128, tl, 64])
                ltjv = lt64_bc[:, bass.ts(j, 64)].unsqueeze(1).broadcast_to([128, tl, 64])
                for opi, opf in enumerate([
                        lambda: v.tensor_tensor(e3, mjv, m3, ALU.is_gt),
                        lambda: v.tensor_tensor(n3, n3, e3, ALU.add),
                        lambda: v.tensor_tensor(e3, mjv, m3, ALU.is_equal),
                        lambda: v.tensor_tensor(e3, e3, ltjv, ALU.mult),
                        lambda: v.tensor_tensor(n3, n3, e3, ALU.add)]):
                    v.wait_ge(sem["s_ve"], ve_reg)
                    v.reg_add(ve_reg, ve_reg, 1)
                    opf().then_inc(sem["s_ve"], 1)
            C["s_ve"].inc(64 * 5)
            v.free_register(ve_reg)
            V(lambda: v.tensor_scalar(rnk[:], rnk[:], 8.0, 0.0, ALU.is_lt, ALU.add))
            V(lambda: v.tensor_tensor(rtmp2[:], sc_tm[:], rnk[:], ALU.mult))
            tw3 = rtmp2.rearrange("p (a e) -> p a e", a=tl)
            V(lambda: v.tensor_reduce(gsml[:, 0:8].unsqueeze(2), tw3, AX.X, ALU.add))
            V(lambda: v.tensor_scalar(gsml[:, 0:8], gsml[:, 0:8], 1e-20, 0.0, ALU.add, ALU.add))
            V(lambda: v.reciprocal(gsml[:, 0:8], gsml[:, 0:8]))
            recb = gsml[:, 0:8].unsqueeze(2).broadcast_to([128, tl, 64])
            V(lambda: v.tensor_tensor(tw3, tw3, recb, ALU.mult))
            V(lambda: v.tensor_scalar(rtmp2[:], rtmp2[:], 1.25, 0.0, ALU.mult, ALU.add))
            kb.wait(t, "s_ve")
            kb.wait(s, "d_st")
            for a in range(tl):
                kb.pe_t(c_fmb[0:64, bass.ts(a, 128)], rtmp2[:, bass.ts(a, 64)])
            kb.wait(v, "s_sc")
            kb.wait(v, "d_st")
            kb.inc("v", v.memset(c_fmb[64:68, :], 0.5), "s_ve")
            kb.wait(sy, "s_ve")
            kb.wait(sy, "s_sc")
            kb.inc("sy", sy.dma_start(out=c_stage[:, half * HALF:(half + 1) * HALF],
                                      in_=c_fmb[:]), "d_st", 16)
            nc.all_engine_barrier()

        def ffn_half(half, pd, pc, tag="A"):
            pu = [kb.ptr[0], kb.ptr[1], kb.pbc[0], kb.pbc[1]]
            PE0, SC0, VE0 = C["s_pe"].v, C["s_sc"].v, C["s_ve"].v
            DWU0, DWD0, DC0 = C["d_wu"].v, C["d_wd"].v, C["d_c"].v
            h0 = half * HALF

            def LC(eng, name, init):
                r = eng.alloc_register(f"{name}_{tag}{half}")
                eng.reg_mov(r, init)
                return r

            def wb(eng, r, semname, bump):
                eng.wait_ge(sem[semname], r)
                if bump:
                    eng.reg_add(r, r, bump)

            rg_wu = LC(g, "rg_wu", PE0 - 8)
            rg_wd = LC(g, "rg_wd", PE0)
            rg_cr = LC(g, "rg_cr", PE0 - 24)
            rt_dc = LC(t, "rt_dc", DC0 + 16)
            rt_dwu = LC(t, "rt_dwu", DWU0 + 16)
            rt_dwd = LC(t, "rt_dwd", DWD0 + 16)
            rt_pc = LC(t, "rt_pc", SC0 - 1)
            rt_pu = LC(t, "rt_pu", SC0 + 6)
            rt_g2 = LC(t, "rt_g2", VE0 + 64)
            rt_pd = LC(t, "rt_pd", VE0 + 66)
            rs_bc = LC(s, "rs_bc", PE0 + 1)
            rs_up = LC(s, "rs_up", PE0 + 9)
            rs_rbf = LC(s, "rs_rbf", VE0 - 4)
            rs_cbw = LC(s, "rs_cbw", VE0 - 16)
            rv_sc = LC(v, "rv_sc", SC0 + 10)
            rv_dn = LC(v, "rv_dn", PE0 + 25)
            rv_self = LC(v, "rv_self", VE0)

            g.wait_ge(sem["d_st"], C["d_st"].v)
            with fori(nc, 0, NQUAD) as qd:
                up_q = all_up[bass.ts(qd, 4)]
                dn_q = all_dn[bass.ts(qd, 4)]
                cs_q = c_stage[bass.ts(qd, 4)]
                wb(g, rg_cr, "s_pe", 32)
                g.dma_start(
                    out=crow_s.rearrange("o (e k) -> o e k", e=4),
                    in_=cs_q[:, h0:h0 + HALF].unsqueeze(0)
                ).then_inc(sem["d_c"], 16)
                wb(g, rg_wu, "s_pe", 32)
                g.dma_start(
                    out=wu_s[:].rearrange("p (e c i) -> p e c i", e=4, c=NDHI),
                    in_=up_q.rearrange("e (c p) i -> p e c i", p=128)
                ).then_inc(sem["d_wu"], 16)
                wb(g, rg_wd, "s_pe", 32)
                g.dma_start(
                    out=wd_s[:].rearrange("p (e c i) -> p e c i", e=4, c=4),
                    in_=dn_q.rearrange("e (c p) i -> p e c i", p=128)
                ).then_inc(sem["d_wd"], 16)

                wb(t, rt_dc, "d_c", 16)
                wb(s, rs_cbw, "s_ve", 80)
                for j in range(8):
                    eu, q = j // 2, j % 2
                    wb(t, rt_pc, "s_sc", 1)
                    t.matmul(pc[j % 2][:], ones_b[:],
                             crow_s[0:1, eu * HALF + q * QW: eu * HALF + (q + 1) * QW],
                             start=True, stop=True).then_inc(sem["s_pe"], 1)
                    wb(s, rs_bc, "s_pe", 1)
                    s.copy(cbc_s[:, eu * HALF + q * QW: eu * HALF + (q + 1) * QW],
                           pc[j % 2][:]).then_inc(sem["s_sc"], 1)
                wb(t, rt_dwu, "d_wu", 16)
                for eu in range(4):
                    for ic in range(4):
                        u = 4 * eu + ic
                        icp = ic % 2
                        wb(t, rt_pu, "s_sc", 2)
                        for dc in range(NDHI):
                            for q in range(2):
                                mm = t.matmul(
                                    pu[2 * icp + q][:],
                                    wu_s[:, eu * (NDHI * 512) + dc * 512 + ic * 128:
                                         eu * (NDHI * 512) + dc * 512 + ic * 128 + 128],
                                    xlnb[:, dc * HALF + q * QW: dc * HALF + (q + 1) * QW],
                                    start=(dc == 0), stop=(dc == NDHI - 1))
                        mm.then_inc(sem["s_pe"], 1)
                        wb(s, rs_up, "s_pe", 1)
                        wb(s, rs_rbf, "s_ve", 4)
                        for q in range(2):
                            s.activation(r_bf[:, (2 * icp + q) * QW:(2 * icp + q + 1) * QW],
                                         pu[2 * icp + q][:], AF.Relu
                                         ).then_inc(sem["s_sc"], 1)
                        wb(v, rv_sc, "s_sc", 2)
                        wb(v, rv_self, "s_ve", 0)
                        for q in range(2):
                            rsl = r_bf[:, (2 * icp + q) * QW:(2 * icp + q + 1) * QW]
                            v.tensor_tensor(rc_bf[:, bass.ts(q, QW)], rsl,
                                            cbc_s[:, eu * HALF + q * QW: eu * HALF + (q + 1) * QW],
                                            ALU.mult).then_inc(sem["s_ve"], 1)
                            v.reg_add(rv_self, rv_self, 1)
                            v.wait_ge(sem["s_ve"], rv_self)
                            v.tensor_tensor(
                                g2_s[:, eu * (4 * HALF) + ic * HALF + q * QW:
                                     eu * (4 * HALF) + ic * HALF + (q + 1) * QW],
                                rc_bf[:, bass.ts(q, QW)], rsl, ALU.mult).then_inc(sem["s_ve"], 1)
                            v.reg_add(rv_self, rv_self, 1)
                wb(t, rt_g2, "s_ve", 80)
                wb(t, rt_dwd, "d_wd", 16)
                for dc in range(NDHI):
                    if dc >= 1:
                        wb(t, rt_pd, "s_ve", 2)
                    for eu in range(4):
                        for ic in range(4):
                            first = (eu == 0 and ic == 0)
                            last = (eu == 3 and ic == 3)
                            for q in range(2):
                                mm = t.matmul(
                                    pd[q][:],
                                    wd_s[:, eu * (4 * DIM) + ic * DIM + dc * 128:
                                         eu * (4 * DIM) + ic * DIM + dc * 128 + 128],
                                    g2_s[:, eu * (4 * HALF) + ic * HALF + q * QW:
                                         eu * (4 * HALF) + ic * HALF + (q + 1) * QW],
                                    start=first, stop=last)
                    mm.then_inc(sem["s_pe"], 1)
                    wb(v, rv_dn, "s_pe", 1)
                    wb(v, rv_self, "s_ve", 0)
                    for q in range(2):
                        v.tensor_tensor(xs3[:, dc, h0 + q * QW: h0 + (q + 1) * QW],
                                        xs3[:, dc, h0 + q * QW: h0 + (q + 1) * QW],
                                        pd[q][:], ALU.add).then_inc(sem["s_ve"], 1)
                        v.reg_add(rv_self, rv_self, 1)
                t.reg_add(rt_pd, rt_pd, 66)
                t.reg_add(rt_pc, rt_pc, 32)
                t.reg_add(rt_pu, rt_pu, 8)
                s.reg_add(rs_bc, rs_bc, 24)
                s.reg_add(rs_up, rs_up, 16)
                s.reg_add(rs_rbf, rs_rbf, 16)
                v.reg_add(rv_sc, rv_sc, 8)
                v.reg_add(rv_dn, rv_dn, 24)
            for eng_, r_ in [(g, rg_wu), (g, rg_wd), (g, rg_cr), (t, rt_dc), (t, rt_dwu),
                             (t, rt_dwd), (t, rt_pc), (t, rt_pu), (t, rt_g2), (t, rt_pd),
                             (s, rs_bc), (s, rs_up), (s, rs_rbf), (s, rs_cbw),
                             (v, rv_sc), (v, rv_dn), (v, rv_self)]:
                eng_.free_register(r_)
            C["s_pe"].inc(32 * NQUAD)
            C["s_sc"].inc(40 * NQUAD)
            C["s_ve"].inc(80 * NQUAD)
            C["d_wu"].inc(16 * NQUAD)
            C["d_wd"].inc(16 * NQUAD)
            C["d_c"].inc(16 * NQUAD)
            nc.all_engine_barrier()


        def flush_and_zero():
            for smn in ["d_sy", "d_g", "d_wu", "d_wd", "d_c", "d_st", "d_out"]:
                g.wait_ge(sem[smn], C[smn].v)
            nc.all_engine_barrier()
            for eng_, smn in [(t, "s_pe"), (s, "s_sc"), (v, "s_ve"), (g, "s_gp"),
                              (sy, "d_sy"), (sy, "d_st"), (sy, "d_out")]:
                eng_.sem_clear(sem[smn])
            nc.all_engine_barrier()
            for smn in C:
                if smn not in ("d_g", "d_wu", "d_wd", "d_c"):
                    C[smn].v = 0

        def mamba_phase():
            build_opwT()
            z_q = wu_s[:, NDHI * DIM:].bitcast(F32)
            sp1 = z_q[:, 2048:2560]
            cbcx = scratch1[:, 0:2048].bitcast(BF16)
            brow = scratch1[0:1, 2048:4096].bitcast(BF16)
            wq = scratch1[:, 4096:6144].bitcast(BF16)
            dtq = scratch1[:, 6144:8192].bitcast(BF16)
            xnqb = wu_s[:, 8192:12288]
            zDq = scratch2[:, 0:2048].bitcast(BF16)
            yq = scratch2[:, 0:2048].bitcast(BF16)
            t1b = wd_s[:, 2560:6656]
            dab = wd_s[:, 6656:10752]
            ub = wd_s[:, 10752:14848]
            Xb = [t1b, wu_s[:, 8192:12288]]
            Db = [dab, wu_s[:, 12288:16384]]
            dlo_b = wd_s[0:64, 14848:15360]
            bfm = wd_s[:, 15360:16384].bitcast(F32)[0:64, :]
            cfm = scratch2[0:64, 2048:2560]
            hb = xlnb[:, 0:4096]
            bbc = xlnb[:, 4096:8192]
            with nc.psum_tensor("pmA", [128, 512], F32) as pmA, \
                 nc.psum_tensor("pmB", [64, 512], F32) as pmB, \
                 nc.psum_tensor("pmst", [1, 2 * QW], F32) as pmst:
                for qt in range(4):
                    xq = xs3[:, :, qt * QW:(qt + 1) * QW]
                    z3 = z_q.rearrange("p (h k) -> p h k", h=NDHI)
                    zsq = scratch2[:, 0:2048]
                    kb.wait(v, "s_sc")
                    kb.wait(v, "s_pe")
                    kb.wait(v, "s_ve")
                    kb.inc("v", v.tensor_copy(z3, xq), "s_ve")
                    for hhalf in range(2):
                        zs = z_q[:, bass.ts(hhalf, 2048)]
                        kb.wait(v, "s_ve")
                        kb.wait(v, "s_pe")
                        kb.inc("v", v.tensor_tensor(zsq, zs, zs, ALU.mult), "s_ve")
                        kb.wait(t, "s_ve")
                        for dc in range(4):
                            mm = t.matmul(pmst[0:1, QW:2 * QW], onescol_f[:],
                                          zsq[:, bass.ts(dc, QW)],
                                          start=(hhalf == 0 and dc == 0),
                                          stop=(hhalf == 1 and dc == 3))
                        kb.inc("t", mm, "s_pe")
                    kb.wait(t, "s_ve")
                    for dc in range(NDHI):
                        mm = t.matmul(pmst[0:1, 0:QW], onescol_f[:], z_q[:, bass.ts(dc, QW)],
                                      start=(dc == 0), stop=(dc == NDHI - 1))
                    kb.inc("t", mm, "s_pe")
                    kb.wait(s, "s_pe")
                    kb.inc("s", s.mul(stat[0:1, 0:QW], pmst[0:1, 0:QW], 1.0 / DIM), "s_sc")
                    kb.inc("s", s.mul(stat[0:1, QW:2 * QW], pmst[0:1, QW:2 * QW], 1.0 / DIM), "s_sc")
                    kb.wait(v, "s_sc")
                    kb.inc("v", v.tensor_tensor(stat[0:1, 2 * QW:3 * QW], stat[0:1, 0:QW],
                                                stat[0:1, 0:QW], ALU.mult), "s_ve")
                    kb.wait(v, "s_ve")
                    kb.inc("v", v.tensor_tensor(stat[0:1, QW:2 * QW], stat[0:1, QW:2 * QW],
                                                stat[0:1, 2 * QW:3 * QW], ALU.subtract), "s_ve")
                    kb.wait(s, "s_ve")
                    kb.inc("s", s.activation(stat[0:1, QW:2 * QW], stat[0:1, QW:2 * QW],
                                             AF.Ln, bias=EPS_LN), "s_sc")
                    kb.wait(s, "s_sc")
                    kb.inc("s", s.activation(stat[0:1, QW:2 * QW], stat[0:1, QW:2 * QW],
                                             AF.Exp, scale=-0.5), "s_sc")
                    kb.wait(t, "s_sc")
                    kb.bc_row(mean_bc[:], stat[0:1, 0:QW])
                    kb.bc_row(rstd_bc[:], stat[0:1, QW:2 * QW])
                    mb = mean_bc[:].unsqueeze(1).broadcast_to([128, NDHI, QW])
                    rb = rstd_bc[:].unsqueeze(1).broadcast_to([128, NDHI, QW])
                    kb.wait(v, "s_sc")
                    kb.inc("v", v.tensor_tensor(z3, z3, mb, ALU.subtract), "s_ve")
                    kb.wait(v, "s_ve")
                    kb.inc("v", v.tensor_tensor(z3, z3, rb, ALU.mult), "s_ve")
                    for hhalf in range(2):
                        zs = z_q[:, bass.ts(hhalf, 2048)]
                        kb.wait(v, "s_ve")
                        kb.wait(v, "s_pe")
                        kb.inc("v", v.tensor_tensor(zsq, zs, zs, ALU.mult), "s_ve")
                        kb.wait(t, "s_ve")
                        for dc in range(4):
                            mm = t.matmul(pmst[0:1, 0:QW], onescol_f[:], zsq[:, bass.ts(dc, QW)],
                                          start=(hhalf == 0 and dc == 0),
                                          stop=(hhalf == 1 and dc == 3))
                        kb.inc("t", mm, "s_pe")
                    kb.wait(s, "s_pe")
                    kb.inc("s", s.mul(stat[0:1, 0:QW], pmst[0:1, 0:QW], 1.0 / DIM), "s_sc")
                    kb.wait(s, "s_sc")
                    kb.inc("s", s.activation(stat[0:1, 0:QW], stat[0:1, 0:QW],
                                             AF.Ln, bias=EPS_RMS), "s_sc")
                    kb.wait(s, "s_sc")
                    kb.inc("s", s.activation(stat[0:1, 0:QW], stat[0:1, 0:QW],
                                             AF.Exp, scale=-0.5), "s_sc")
                    kb.wait(t, "s_sc")
                    kb.bc_row(rstd_bc[:], stat[0:1, 0:QW])
                    dvb = dvec_fm[:].unsqueeze(2).broadcast_to([128, NDHI, QW])
                    zD3 = zDq.rearrange("p (h k) -> p h k", h=NDHI)
                    kb.wait(v, "s_sc")
                    kb.inc("v", v.tensor_tensor(zD3, z3, dvb, ALU.mult), "s_ve")
                    kb.wait(sy, "s_ve")
                    kb.inc("sy", sy.dma_start(out=zd_stage, in_=zDq), "d_st", 16)
                    rb2 = rstd_bc[:].unsqueeze(1).broadcast_to([128, NDHI, QW])
                    kb.wait(v, "s_ve")
                    kb.inc("v", v.tensor_tensor(z3, z3, rb2, ALU.mult), "s_ve")
                    nwb = nw_fm[:].unsqueeze(2).broadcast_to([128, NDHI, QW])
                    xnb3 = xnqb.rearrange("p (h k) -> p h k", h=NDHI)
                    kb.wait(v, "s_ve")
                    kb.inc("v", v.tensor_tensor(xnb3, z3, nwb, ALU.mult), "s_ve")
                    kb.wait(t, "s_ve")
                    kb.wait(t, "s_sc")
                    for dc in range(NDHI):
                        mm = t.matmul(pmA[:], xpwT[:, dc * 192: dc * 192 + 128],
                                      xnqb[:, bass.ts(dc, QW)],
                                      start=(dc == 0), stop=(dc == NDHI - 1))
                    kb.inc("t", mm, "s_pe")
                    kb.wait(s, "s_pe")
                    kb.wait(s, "d_st")
                    kb.inc("s", s.copy(dlo_b, pmA[0:64, :]), "s_sc")
                    kb.inc("s", s.copy(bfm, pmA[64:128, :]), "s_sc")
                    kb.wait(t, "s_sc")
                    for dc in range(NDHI):
                        mm = t.matmul(pmB[:], xpwT[:, dc * 192 + 128: dc * 192 + 192],
                                      xnqb[:, bass.ts(dc, QW)],
                                      start=(dc == 0), stop=(dc == NDHI - 1))
                    kb.inc("t", mm, "s_pe")
                    kb.wait(s, "s_pe")
                    kb.inc("s", s.copy(cfm, pmB[:]), "s_sc")
                    kb.wait(sy, "s_sc")
                    kb.wait(sy, "d_c")
                    kb.inc("sy", sy.dma_start(
                        out=b_stage.rearrange("m (n tt) -> n m tt", n=STATE),
                        in_=bfm.rearrange("n (m tt) -> n m tt", m=8)), "d_st", 16)
                    kb.inc("sy", sy.dma_start(
                        out=cc_stage.rearrange("m (n tt) -> n m tt", n=STATE),
                        in_=cfm.rearrange("n (m tt) -> n m tt", m=8)), "d_st", 16)
                    for dc in range(NDHI):
                        kb.wait(t, "s_sc")
                        mm = t.matmul(pmA[:], dtpwT[:, bass.ts(dc, 128)], dlo_b,
                                      start=True, stop=True)
                        kb.inc("t", mm, "s_pe")
                        bcol = dtb_fm[:, dc:dc + 1]
                        kb.wait(s, "s_pe")
                        kb.wait(s, "s_ve")
                        kb.inc("s", s.activation(sp1, pmA[:], AF.Abs, bias=bcol), "s_sc")
                        kb.wait(s, "s_sc")
                        kb.inc("s", s.activation(sp1, sp1, AF.Exp, scale=-1.0), "s_sc")
                        kb.wait(s, "s_sc")
                        kb.inc("s", s.activation(sp1, sp1, AF.Ln, bias=1.0), "s_sc")
                        kb.inc("s", s.activation(z_q[:, 2560:3072], pmA[:], AF.Relu, bias=bcol), "s_sc")
                        kb.wait(v, "s_sc")
                        kb.inc("v", v.tensor_tensor(dtq[:, bass.ts(dc, QW)], sp1,
                                                    z_q[:, 2560:3072], ALU.add), "s_ve")
                    kb.wait(v, "s_ve")
                    kb.inc("v", v.tensor_tensor(wq, dtq, xnqb, ALU.mult), "s_ve")
                    kb.wait(v, "d_st")
                    scan_m(pmA, brow, bbc, cbcx, dtq, wq, Xb, Db, ub, hb, yq)
                    kb.wait(sy, "s_ve")
                    kb.inc("sy", sy.dma_start(out=hb, in_=zd_stage), "d_st", 16)
                    kb.wait(v, "d_st")
                    kb.wait(v, "s_ve")
                    kb.wait(v, "s_gp")
                    kb.inc("v", v.tensor_tensor(yq, yq, hb, ALU.add), "s_ve")
                    oq = z_q
                    for oc in range(NDHI):
                        kb.wait(t, "s_ve")
                        kb.wait(t, "s_sc")
                        for dc in range(NDHI):
                            mm = t.matmul(pmA[:], opwT[:, dc * DIM + oc * 128: dc * DIM + oc * 128 + 128],
                                          yq[:, bass.ts(dc, QW)],
                                          start=(dc == 0), stop=(dc == NDHI - 1))
                        kb.inc("t", mm, "s_pe")
                        kb.wait(s, "s_pe")
                        kb.wait(s, "s_ve")
                        kb.inc("s", s.copy(oq[:, bass.ts(oc, QW)], pmA[:]), "s_sc")
                    kb.wait(v, "s_sc")
                    oq3 = oq.rearrange("p (h k) -> p h k", h=NDHI)
                    kb.inc("v", v.tensor_tensor(xq, xq, oq3, ALU.add), "s_ve")
            kb.wait(sy, "s_pe")
            kb.wait(sy, "s_ve")
            kb.wait(sy, "s_sc")
            kb.wait(sy, "d_st")
            kb.inc("sy", sy.dma_start(out=scratch2[0:1, 0:4096],
                                      in_=lt64_in.rearrange("a b -> (a b)").unsqueeze(0)),
                   "d_st", 16)
            kb.wait(t, "s_ve")
            kb.wait(t, "d_st")
            kb.wait(s, "s_ve")
            for j in range(8):
                kb.bc_row(lt64_bc[:, bass.ts(j, 512)], scratch2[0:1, bass.ts(j, 512)])
            nc.all_engine_barrier()

        def scan_m(pmA, brow, bbc, cbcx, dtq, wq, Xb, Db, ub, hb, yq):
            PE0, SC0, VE0, GP0, DC0 = (C["s_pe"].v, C["s_sc"].v, C["s_ve"].v,
                                       C["s_gp"].v, C["d_c"].v)
            pbk = [kb.ptr[0], kb.ptr[1], kb.pbc[0], kb.pbc[1]]
            kb.wait(g, "d_st")
            kb.wait(s, "d_st")
            kb.wait(v, "s_pe")
            ub3 = ub.rearrange("p (n tt) -> p n tt", n=STATE)
            bbc3 = bbc.rearrange("p (n tt) -> p n tt", n=STATE)
            axp = scratch2[:, 2048:4096].bitcast(BF16)
            axp3 = axp.rearrange("p (n tt) -> p n tt", n=STATE)
            dte = scratch2[:, 4096:6144].bitcast(BF16)
            dte3 = dte.rearrange("p (n tt) -> p n tt", n=STATE)

            def BS(m):
                return SC0 + 2 + 40 * m

            def exp_done(tau):
                return BS(tau // 8) + 16 + 3 * (tau % 8) + 1

            def copies_done(tau):
                if tau == 0:
                    return SC0 + 2
                return BS((tau - 1) // 8) + 16 + 3 * ((tau - 1) % 8) + 3

            def emit_copies(tau):
                m, dhi = tau // 8, tau % 8
                o0 = dhi * QW + m * CHUNK
                a_ap = a_fmb[:, dhi * STATE:(dhi + 1) * STATE].unsqueeze(2) \
                    .broadcast_to([128, STATE, 64])
                dt_ap = dtq[:, o0:o0 + CHUNK].unsqueeze(1).broadcast_to([128, STATE, 64])
                if tau >= 1:
                    s.wait_ge(sem["s_ve"], VE0 + t1_done[tau - 1])
                s.copy(axp3, a_ap).then_inc(sem["s_sc"], 1)
                s.copy(dte3, dt_ap).then_inc(sem["s_sc"], 1)

            t1_done, scan_done, hc_done = {}, {}, {}
            _k = 0
            t1_done[0] = _k + 1; _k += 1
            scan_done[0] = _k + 1; _k += 1
            for _tau in range(64):
                if _tau + 1 < 64:
                    t1_done[_tau + 1] = _k + 1; _k += 1
                hc_done[_tau] = _k + 1; _k += 1
                _k += 6
                if _tau + 1 < 64:
                    scan_done[_tau + 1] = _k + 1; _k += 1
            assert _k == 9 * 64
            emit_copies(0)

            for m in range(8):
                g.wait_ge(sem["s_pe"], PE0 + 16 * m)
                g.dma_start(out=brow, in_=b_stage[m].unsqueeze(0)).then_inc(sem["d_c"], 16)
                g.wait_ge(sem["s_pe"], PE0 + 16 * m + 8)
                g.dma_start(out=brow, in_=cc_stage[m].unsqueeze(0)).then_inc(sem["d_c"], 16)
                t.wait_ge(sem["d_c"], DC0 + 32 * m + 16)
                for j in range(8):
                    tgt = BS(m) + j - 3 if j >= 4 else max(BS(m) - 24, 0)
                    t.wait_ge(sem["s_sc"], tgt)
                    t.matmul(pbk[j % 4][:], ones_b[:], brow[0:1, bass.ts(j, 512)],
                             start=True, stop=True).then_inc(sem["s_pe"], 1)
                    if j == 0:
                        s.wait_ge(sem["s_gp"], max(GP0 + 16 * m - 1, 0))
                    s.wait_ge(sem["s_pe"], PE0 + 16 * m + j + 1)
                    s.copy(bbc[:, bass.ts(j, 512)], pbk[j % 4][:]).then_inc(sem["s_sc"], 1)
                t.wait_ge(sem["d_c"], DC0 + 32 * m + 32)
                for j in range(8):
                    t.wait_ge(sem["s_sc"], BS(m) + j + 5)
                    t.matmul(pbk[j % 4][:], ones_b[:], brow[0:1, bass.ts(j, 512)],
                             start=True, stop=True).then_inc(sem["s_pe"], 1)
                    if j == 0:
                        s.wait_ge(sem["s_ve"],
                                  VE0 + hc_done[8 * m - 1] if m >= 1 else max(VE0, 0))
                    s.wait_ge(sem["s_pe"], PE0 + 16 * m + 8 + j + 1)
                    s.copy(cbcx[:, bass.ts(j, 512)], pbk[j % 4][:]).then_inc(sem["s_sc"], 1)
                for dhi in range(NDHI):
                    tau = 8 * m + dhi
                    o0 = dhi * QW + m * CHUNK
                    w_ap = wq[:, o0:o0 + CHUNK].unsqueeze(1).broadcast_to([128, STATE, 64])
                    s.wait_ge(sem["s_ve"], VE0 + t1_done[tau])
                    s.activation(Db[tau % 2], Xb[tau % 2], AF.Exp).then_inc(sem["s_sc"], 1)
                    if tau + 1 < 64:
                        emit_copies(tau + 1)
                    g.wait_ge(sem["s_sc"], BS(m) + 8)
                    g.wait_ge(sem["s_ve"], VE0 + (scan_done[tau - 1] if tau >= 1 else 0))
                    g.tensor_tensor(ub3, w_ap, bbc3, ALU.mult).then_inc(sem["s_gp"], 1)
                    g.wait_ge(sem["s_sc"], exp_done(tau))
                    g.memset(Db[tau % 2].rearrange("p (n tt) -> p n tt", n=STATE)[:, :, 0:1],
                             0.0).then_inc(sem["s_gp"], 1)
            vk = [0]

            def vpre():
                v.wait_ge(sem["s_ve"], VE0 + vk[0])
                vk[0] += 1

            def emit_t1(tau):
                m, dhi = tau // 8, tau % 8
                o0 = dhi * QW + m * CHUNK
                x3 = Xb[tau % 2].rearrange("p (n tt) -> p n tt", n=STATE)
                v.wait_ge(sem["s_sc"], copies_done(tau))
                vpre()
                v.tensor_tensor(x3, axp3, dte3, ALU.mult).then_inc(sem["s_ve"], 1)

            def emit_scan(tau):
                v.wait_ge(sem["s_gp"], GP0 + 2 * tau + 2)
                v.wait_ge(sem["s_sc"], exp_done(tau))
                vpre()
                v.tensor_tensor_scan(hb, Db[tau % 2], ub, 0.0, ALU.mult, ALU.add
                                     ).then_inc(sem["s_ve"], 1)

            def emit_hc_tree(tau):
                m, dhi = tau // 8, tau % 8
                o0 = dhi * QW + m * CHUNK
                Xf = Xb[tau % 2]
                v.wait_ge(sem["s_sc"], BS(m) + 16)
                vpre()
                v.tensor_tensor(Xf, hb, cbcx, ALU.mult).then_inc(sem["s_ve"], 1)
                L = 2048
                for j in range(5):
                    vpre()
                    v.tensor_tensor(Xf[:, 0:L], Xf[:, 0:L], Xf[:, L:2 * L], ALU.add
                                    ).then_inc(sem["s_ve"], 1)
                    L //= 2
                vpre()
                v.tensor_tensor(yq[:, o0:o0 + CHUNK], Xf[:, 0:CHUNK],
                                Xf[:, CHUNK:2 * CHUNK], ALU.add).then_inc(sem["s_ve"], 1)

            emit_t1(0)
            emit_scan(0)
            for tau in range(64):
                if tau + 1 < 64:
                    emit_t1(tau + 1)
                emit_hc_tree(tau)
                if tau + 1 < 64:
                    emit_scan(tau + 1)
            assert vk[0] == 9 * 64
            C["s_pe"].inc(16 * 8)
            C["s_sc"].inc(40 * 8)
            C["s_ve"].inc(9 * 64)
            C["s_gp"].inc(2 * 64)
            C["d_c"].inc(32 * 8)

        def dump_stream(dst):
            for blk in range(16):
                tok0 = blk * 128
                kb.wait(v, "d_out")
                kb.wait(v, "s_sc")
                kb.inc("v", v.tensor_copy(
                    tm_tile.rearrange("p (h k) -> p h k", h=NDHI),
                    xs3[:, :, tok0:tok0 + 128]), "s_ve")
                kb.wait(t, "s_ve")
                for dc in range(NDHI):
                    kb.pe_t(tm_tile[:, bass.ts(dc, 128)], tm_tile[:, bass.ts(dc, 128)])
                kb.wait(sy, "s_sc")
                kb.inc("sy", sy.dma_start(out=dst[tok0:tok0 + 128, :],
                                          in_=tm_tile), "d_out", 16)
            sy.wait_ge(sem["d_out"], C["d_out"].v)
            nc.all_engine_barrier()

        setup()
        for half in range(2):
            load_x_half(half)
            with nc.psum_tensor(f"pstat{half}", [1, 2 * QW], F32) as pst, \
                 nc.psum_tensor(f"plogit{half}", [64, 512], F32) as plg:
                routing_half(half, pst, plg)
            if stage >= 3:
                with nc.psum_tensor(f"pd0_{half}", [128, 512], F32) as pd0, \
                     nc.psum_tensor(f"pd1_{half}", [128, 512], F32) as pd1, \
                     nc.psum_tensor(f"pc0_{half}", [128, 512], F32) as pc0, \
                     nc.psum_tensor(f"pc1_{half}", [128, 512], F32) as pc1:
                    ffn_half(half, [pd0, pd1], [pc0, pc1])

        if stage >= 4:
            mamba_phase()

        if stage >= 5:
            for half in range(2):
                with nc.psum_tensor(f"pstatB{half}", [1, 2 * QW], F32) as pst, \
                     nc.psum_tensor(f"plogitB{half}", [64, 512], F32) as plg:
                    routing_half(half, pst, plg, tag="B")
                with nc.psum_tensor(f"pdB0_{half}", [128, 512], F32) as pd0, \
                     nc.psum_tensor(f"pdB1_{half}", [128, 512], F32) as pd1, \
                     nc.psum_tensor(f"pcB0_{half}", [128, 512], F32) as pc0, \
                     nc.psum_tensor(f"pcB1_{half}", [128, 512], F32) as pc1:
                    ffn_half(half, [pd0, pd1], [pc0, pc1], tag="B")
            dump_stream(out)

        if stage in (3, 4):
            dump_stream(dbg)
            kb.wait(v, "d_out")
            kb.inc("v", v.memset(tm_tile, 0.0), "s_ve")
            kb.wait(sy, "s_ve")
            for blk in range(16):
                kb.inc("sy", sy.dma_start(out=out[blk * 128:(blk + 1) * 128, :],
                                          in_=tm_tile), "d_out", 16)
            kb.inc("g", g.dma_start(out=dbg_c, in_=c_stage), "d_g", 16)
            sy.wait_ge(sem["d_out"], C["d_out"].v)
            g.wait_ge(sem["d_g"], C["d_g"].v)

        if stage <= 2:
            with nc.psum_tensor("pdbg", [128, 512], F32) as pdbg:
                for blk in range(8):
                    tok0 = blk * 128
                    kb.wait(v, "d_out")
                    xl3 = xlnb[:].rearrange("p (h k) -> p h k", h=NDHI)[:, :, tok0:tok0 + 128]
                    kb.wait(v, "s_sc")
                    kb.inc("v", v.tensor_copy(
                        tm_tile.rearrange("p (h k) -> p h k", h=NDHI), xl3), "s_ve")
                    kb.wait(t, "s_ve")
                    for dc in range(NDHI):
                        kb.pe_t(tm_tile[:, bass.ts(dc, 128)], tm_tile[:, bass.ts(dc, 128)])
                    kb.wait(sy, "s_sc")
                    kb.inc("sy", sy.dma_start(out=dbg[HALF + tok0:HALF + tok0 + 128, :],
                                              in_=tm_tile), "d_out", 16)
            g.wait_ge(sem["d_st"], C["d_st"].v)
            kb.inc("g", g.dma_start(out=dbg_c, in_=c_stage), "d_g", 16)
            kb.wait(v, "d_out")
            kb.inc("v", v.memset(tm_tile, 0.0), "s_ve")
            kb.wait(sy, "s_ve")
            for blk in range(16):
                kb.inc("sy", sy.dma_start(out=out[blk * 128:(blk + 1) * 128, :],
                                          in_=tm_tile), "d_out", 16)
            kb.inc("sy", sy.dma_start(out=dbg[0:HALF, :], in_=x_in[0:HALF, :]), "d_out", 16)
            sy.wait_ge(sem["d_out"], C["d_out"].v)
            g.wait_ge(sem["d_g"], C["d_g"].v)
    return nc


K_CONST_VALS = CONST_VALS


def make_consts():
    ident = np.eye(128, dtype=np.float32)
    ltg = np.zeros((1, 64), dtype=np.float32)
    for j in range(8):
        for i in range(8):
            if j < i:
                ltg[0, j * 8 + i] = 1.0
    lt64 = np.zeros((64, 64), dtype=np.float32)
    for j in range(64):
        for i in range(64):
            if j < i:
                lt64[j, i] = 1.0
    return ident, ltg, lt64.reshape(8, 512)


def prep_inputs(inputs):
    x = np.ascontiguousarray(np.asarray(inputs["L0_attn_x"]).reshape(-1, DIM))
    up = np.asarray(inputs["L0_ffn_up_w"])
    dn = np.asarray(inputs["L0_ffn_down_w"])
    sup = np.asarray(inputs["L0_ffn_shared_up_w"])
    sdn = np.asarray(inputs["L0_ffn_shared_down_w"])
    all_up_a = np.concatenate(
        [up, np.stack([sup[:, j * 512:(j + 1) * 512] for j in range(4)])], axis=0)
    all_dn_a = np.concatenate([dn, sdn.reshape(4, 512, DIM)], axis=0)
    ident, ltg, lt64 = make_consts()
    base = dict(
        all_up=np.ascontiguousarray(all_up_a),
        all_dn=np.ascontiguousarray(all_dn_a),
        router_w=np.ascontiguousarray(np.asarray(inputs["L0_ffn_router_w"])),
        cb_in=np.ascontiguousarray(np.asarray(inputs["L0_ffn_correction_bias"]).reshape(1, NE)),
        xpw_in=np.ascontiguousarray(np.asarray(inputs["L0_attn_x_proj_w"])),
        dtpw_in=np.ascontiguousarray(np.asarray(inputs["L0_attn_dt_proj_w"])),
        dtb_in=np.ascontiguousarray(np.asarray(inputs["L0_attn_dt_proj_b"]).reshape(NDHI, 128)),
        alog_in=np.ascontiguousarray(np.asarray(inputs["L0_attn_A_log"])),
        dvec_in=np.ascontiguousarray(np.asarray(inputs["L0_attn_D"]).reshape(NDHI, 128)),
        nw_in=np.ascontiguousarray(np.asarray(inputs["L0_attn_norm_w"]).reshape(NDHI, 128)),
        opw_in=np.ascontiguousarray(np.asarray(inputs["L0_attn_out_proj_w"])),
        ident_in=ident, ltg_in=ltg, lt64_in=lt64,
        consts_in=np.tile(np.asarray(K_CONST_VALS, np.float32)[None, :], (128, 1)),
    )
    in_maps = []
    for c in range(NCORE):
        m = dict(base)
        m["x_in"] = np.ascontiguousarray(x[c * TOK:(c + 1) * TOK, :])
        in_maps.append(m)
    return in_maps


def kernel(**inputs):
    in_maps = prep_inputs(inputs)
    nc = build(stage=5)
    res = run_bass_kernel_spmd(nc, in_maps, core_ids=list(range(NCORE)))
    outs = [np.asarray(r["out"], dtype=np.float32) for r in res.results]
    return np.concatenate(outs, axis=0).reshape(4, 4096, DIM)


# version 07
# speedup vs baseline: 1.0534x; 1.0534x over previous
import numpy as np
import ml_dtypes
import concourse.bass as bass
import concourse.mybir as mybir
from concourse.bass_utils import run_bass_kernel_spmd
from contextlib import ExitStack

F32 = mybir.dt.float32
BF16 = mybir.dt.bfloat16
AF = mybir.ActivationFunctionType
ALU = mybir.AluOpType
AX = mybir.AxisListType

NCORE = 8
DIM = 1024
NDHI = 8
TOK = 2048
HALF = 1024
NQ = 2
QW = 512
STATE = 64
CHUNK = 64
NE = 64
NEA = 68
I_MOE = 512
EPS_LN = 1e-5
EPS_RMS = 1e-6
BIGNEG = -1e30
NQUAD = 17
CONST_VALS = [1e-5, 1e-6, 1.0, -0.5, -1.0, 1.0 / 1024.0, 0.5, 2.0, -1e30, 64.0,
              1e-20, 1.25, 0.0, -1.0 / 1024.0, 8.0, 4.0]


from contextlib import contextmanager


@contextmanager
def fori(nc, start, end, step=1):
    import concourse.mybir as _mb
    engines = _mb.ALL_ENGINES
    loop_id = nc.next_id()
    loop_name = f"kfori_{loop_id}"
    loop_start = f"{loop_name}_loop"
    loop_end = f"{loop_name}_end"
    registers = nc.alloc_registers(f"{loop_name}_i", engines=engines)
    nc.regs_mov(registers, start)
    nc.br(loop_start, engines=engines)
    snapped = [None]
    with nc.body(loop_start, valid_engines=engines):
        sv = nc.snap(registers, min_val=start, max_val=end - 1)
        snapped[0] = sv
        yield sv
        nc.regs_alu(registers, registers, step, op=mybir.AluOpType.add)
        nc.br_lt(registers, end, on_true=loop_start, on_false=loop_end, engines=engines)
    nc.switch_bb(loop_end)
    for handles in (registers, getattr(snapped[0], "val", None)):
        try:
            hs = handles.handles if hasattr(handles, "handles") else None
            if hs:
                for h in hs:
                    nc.engines[h.engine].free_register(h)
        except Exception:
            pass


class Ctr:
    def __init__(self):
        self.v = 0

    def inc(self, n=1):
        self.v += n
        return self.v


class KB:

    def __init__(self, nc):
        self.nc = nc
        self.g, self.t, self.s, self.v, self.sy = (
            nc.gpsimd, nc.tensor, nc.scalar, nc.vector, nc.sync)
        self.ctx = ExitStack()
        e_ = self.ctx.enter_context
        self.sem = {}
        self.c = {}
        for name in ["d_sy", "d_g", "d_wu", "d_wd", "d_c", "d_st", "d_out",
                     "s_pe", "s_sc", "s_ve", "s_gp"]:
            self.sem[name] = e_(nc.semaphore(name))
            self.c[name] = Ctr()

    def inc(self, eng, instr, sem, n=1):
        instr.then_inc(self.sem[sem], n)
        self.c[sem].inc(n)

    def wait(self, eng, sem):
        eng.wait_ge(self.sem[sem], self.c[sem].v)

    def init_tr(self, id_f):
        self.id_f = id_f
        nc = self.nc
        self.ptr = [self.ctx.enter_context(nc.psum_tensor(f"ptrT{i}", [128, 512], F32))
                    for i in range(2)]
        self.tr_i = 0

    def pe_t(self, out_ap, in_ap):
        t, s = self.t, self.s
        pi, fi = in_ap.shape
        pb = in_ap.base_partition()
        bank = self.ptr[self.tr_i % 2]
        self.tr_i += 1
        t.wait_ge(self.sem["s_sc"], self.c["s_sc"].v)
        self.inc("t", t.transpose(bank[0:fi, 0:pi], in_ap,
                                  self.id_f[pb:pb + pi, pb:pb + pi]), "s_pe")
        s.wait_ge(self.sem["s_pe"], self.c["s_pe"].v)
        self.inc("s", s.copy(out_ap, bank[0:fi, 0:pi]), "s_sc")

    def init_bc(self, ones_f):
        self.ones_f = ones_f
        self.pbc = [self.ctx.enter_context(self.nc.psum_tensor(f"pbcB{i}", [128, 512], F32))
                    for i in range(2)]
        self.bc_i = 0

    def bc_row(self, out_ap, src_row_ap):
        t, s = self.t, self.s
        w = src_row_ap.shape[1]
        bank = self.pbc[self.bc_i % 2]
        self.bc_i += 1
        t.wait_ge(self.sem["s_sc"], self.c["s_sc"].v)
        self.inc("t", t.matmul(bank[:, 0:w], self.ones_f[:], src_row_ap,
                               start=True, stop=True), "s_pe")
        s.wait_ge(self.sem["s_pe"], self.c["s_pe"].v)
        self.inc("s", s.copy(out_ap, bank[:, 0:w]), "s_sc")


def build(stage=5):
    nc = bass.Bass("TRN2", target_bir_lowering=False)

    x_in = nc.dram_tensor("x_in", [TOK, DIM], F32, kind="ExternalInput").ap()
    all_up = nc.dram_tensor("all_up", [NEA, DIM, I_MOE], F32, kind="ExternalInput").ap()
    all_dn = nc.dram_tensor("all_dn", [NEA, I_MOE, DIM], F32, kind="ExternalInput").ap()
    router_w = nc.dram_tensor("router_w", [NE, DIM], F32, kind="ExternalInput").ap()
    cb_in = nc.dram_tensor("cb_in", [1, NE], F32, kind="ExternalInput").ap()
    xpw_in = nc.dram_tensor("xpw_in", [192, DIM], F32, kind="ExternalInput").ap()
    dtpw_in = nc.dram_tensor("dtpw_in", [DIM, STATE], F32, kind="ExternalInput").ap()
    dtb_in = nc.dram_tensor("dtb_in", [NDHI, 128], F32, kind="ExternalInput").ap()
    alog_in = nc.dram_tensor("alog_in", [DIM, STATE], F32, kind="ExternalInput").ap()
    dvec_in = nc.dram_tensor("dvec_in", [NDHI, 128], F32, kind="ExternalInput").ap()
    nw_in = nc.dram_tensor("nw_in", [NDHI, 128], F32, kind="ExternalInput").ap()
    opw_in = nc.dram_tensor("opw_in", [DIM, DIM], F32, kind="ExternalInput").ap()
    ident_in = nc.dram_tensor("ident_in", [128, 128], F32, kind="ExternalInput").ap()
    ltg_in = nc.dram_tensor("ltg_in", [1, 64], F32, kind="ExternalInput").ap()
    consts_in = nc.dram_tensor("consts_in", [128, 16], F32, kind="ExternalInput").ap()
    lt64_in = nc.dram_tensor("lt64_in", [8, 512], F32, kind="ExternalInput").ap()

    out = nc.dram_tensor("out", [TOK, DIM], F32, kind="ExternalOutput").ap()
    c_stage = nc.dram_tensor("c_stage", [NEA, TOK], BF16, kind="Internal").ap()
    b_stage = nc.dram_tensor("b_stage", [8, STATE * CHUNK], F32, kind="Internal").ap()
    cc_stage = nc.dram_tensor("cc_stage", [8, STATE * CHUNK], F32, kind="Internal").ap()
    zd_stage = nc.dram_tensor("zd_stage", [128, 4096], BF16, kind="Internal").ap()

    dbg = dbg_c = None
    if stage < 5:
        dbg = nc.dram_tensor("dbg", [TOK, DIM], F32, kind="ExternalOutput").ap()
        dbg_c = nc.dram_tensor("dbg_c", [NEA, TOK], F32, kind="ExternalOutput").ap()

    kb = KB(nc)
    g, t, s, v, sy = kb.g, kb.t, kb.s, kb.v, kb.sy
    sem, C = kb.sem, kb.c

    with kb.ctx:
        e_ = kb.ctx.enter_context

        xs = e_(nc.sbuf_tensor("xs", [128, NDHI * TOK], F32))
        id_f = e_(nc.sbuf_tensor("id_f", [128, 128], F32))
        ones_f = e_(nc.sbuf_tensor("ones_f", [1, 128], F32))
        ones_b = e_(nc.sbuf_tensor("ones_b", [1, 128], BF16))
        onescol_f = e_(nc.sbuf_tensor("onescol_f", [128, 1], F32))
        rwT = e_(nc.sbuf_tensor("rwT", [128, NDHI * 64], F32))

        a_fm = e_(nc.sbuf_tensor("a_fm", [128, NDHI * STATE], F32))
        a_fmb = e_(nc.sbuf_tensor("a_fmb", [128, NDHI * STATE], BF16))
        dtb_fm = e_(nc.sbuf_tensor("dtb_fm", [128, NDHI], F32))
        dvec_fm = e_(nc.sbuf_tensor("dvec_fm", [128, NDHI], F32))
        nw_fm = e_(nc.sbuf_tensor("nw_fm", [128, NDHI], F32))
        cb_bc = e_(nc.sbuf_tensor("cb_bc", [128, 64], F32))
        ltg_bc = e_(nc.sbuf_tensor("ltg_bc", [128, 64], F32))

        consts_sb = e_(nc.sbuf_tensor("consts_sb", [128, 16], F32))
        for _ci, _cv in enumerate(CONST_VALS):
            nc.const_aps.register(F32, float(_cv), consts_sb[:, _ci:_ci + 1])
        scratch1 = e_(nc.sbuf_tensor("scratch1", [128, 8192], F32))
        lt64_bc = scratch1[:, 0:4096]
        scratch2 = e_(nc.sbuf_tensor("scratch2", [128, 6144], F32))
        sc_tm = scratch2[:, 0:512]
        sfc = scratch2[:, 512:1024]
        rnk = scratch2[:, 1024:1536]
        rtmp = scratch2[:, 1536:2048]
        rtmp2 = scratch2[:, 2048:2560]
        mean_bc = scratch2[:, 2560:3072]
        rstd_bc = scratch2[:, 3072:3584]
        gsml = scratch2[:, 3584:3648]
        gsml2 = scratch2[:, 3648:3712]
        gsml3 = scratch2[:, 3712:3776]
        gsml4 = scratch2[:, 3776:3840]
        stat = scratch2[0:1, 3840:5376]
        c_fmb = scratch2[:, 5376:5888].bitcast(BF16)[0:NEA, :]
        wu_s = e_(nc.sbuf_tensor("wu_s", [128, 4 * NDHI * 512], BF16))
        opwT = wu_s[:, 0:NDHI * DIM]
        wd_s = e_(nc.sbuf_tensor("wd_s", [128, 4 * 4 * DIM], BF16))
        g2_s = scratch1[:].bitcast(BF16)

        cbc_s = scratch2[:, 0:2048].bitcast(BF16)
        r_bf = scratch2[:, 2048:3072].bitcast(BF16)
        rc_bf = scratch2[:, 3072:3584].bitcast(BF16)
        crow_s = scratch2[0:1, 3584:5632].bitcast(BF16)
        tm_tile = scratch2[:, 0:1024]
        xpwT = wd_s[:, 0:1536]
        dtpwT = wd_s[0:64, 1536:2560]
        xlnb = e_(nc.sbuf_tensor("xlnb", [128, NDHI * HALF], BF16))
        xln32 = scratch1[:, 4096:8192]

        kb.init_tr(id_f)
        kb.init_bc(ones_f)

        xs3 = xs[:].rearrange("p (h k) -> p h k", h=NDHI)

        def setup():
            kb.inc("g", g.dma_start(out=consts_sb[:], in_=consts_in), "d_g", 16)
            kb.inc("g", g.dma_start(out=id_f[:], in_=ident_in), "d_g", 16)
            kb.inc("g", g.dma_start(
                out=a_fm[:].rearrange("p (h n) -> p h n", h=NDHI),
                in_=alog_in.rearrange("(h p) n -> p h n", p=128)), "d_g", 16)
            kb.inc("g", g.dma_start(out=tm_tile[0:8, 0:128], in_=dtb_in), "d_g", 16)
            kb.inc("g", g.dma_start(out=tm_tile[32:40, 0:128], in_=dvec_in), "d_g", 16)
            kb.inc("g", g.dma_start(out=tm_tile[64:72, 0:128], in_=nw_in), "d_g", 16)
            kb.inc("g", g.dma_start(out=tm_tile[0:1, 128:192], in_=cb_in), "d_g", 16)
            kb.inc("g", g.dma_start(out=tm_tile[0:1, 192:256], in_=ltg_in), "d_g", 16)
            kb.inc("g", g.dma_start(out=xs[0:1, 0:4096], in_=lt64_in.rearrange("a b -> (a b)").unsqueeze(0)), "d_g", 16)
            g.wait_ge(sem["d_g"], C["d_g"].v)
            kb.inc("g", g.memset(ones_f[:], 1.0), "s_gp")
            kb.inc("g", g.memset(ones_b[:], 1.0), "s_gp")
            kb.inc("g", g.memset(onescol_f[:], 1.0), "s_gp")
            t.wait_ge(sem["s_gp"], C["s_gp"].v)
            s.wait_ge(sem["s_gp"], C["s_gp"].v)
            v.wait_ge(sem["s_gp"], C["s_gp"].v)
            kb.inc("s", s.activation(a_fm[:], a_fm[:], AF.Exp), "s_sc")
            kb.wait(v, "s_sc")
            kb.inc("v", v.tensor_scalar_mul(a_fm[:], a_fm[:], -1.0), "s_ve")
            kb.wait(v, "s_ve")
            kb.inc("v", v.tensor_copy(a_fmb[:], a_fm[:]), "s_ve")
            kb.pe_t(dtb_fm[:], tm_tile[0:8, 0:128])
            kb.pe_t(dvec_fm[:], tm_tile[32:40, 0:128])
            kb.pe_t(nw_fm[:], tm_tile[64:72, 0:128])
            kb.bc_row(cb_bc[:], tm_tile[0:1, 128:192])
            kb.bc_row(ltg_bc[:], tm_tile[0:1, 192:256])
            for j in range(8):
                kb.bc_row(lt64_bc[:, bass.ts(j, 512)], xs[0:1, bass.ts(j, 512)])
            kb.wait(sy, "s_sc")
            kb.inc("sy", sy.dma_start(out=tm_tile[0:64, :], in_=router_w), "d_sy", 16)
            t.wait_ge(sem["d_sy"], C["d_sy"].v)
            s.wait_ge(sem["d_sy"], C["d_sy"].v)
            for dc in range(NDHI):
                kb.pe_t(rwT[:, bass.ts(dc, 64)], tm_tile[0:64, bass.ts(dc, 128)])
            nc.all_engine_barrier()

        def build_opwT():
            kb.wait(sy, "s_sc")
            kb.inc("sy", sy.dma_start(out=tm_tile, in_=xpw_in[0:128, :]), "d_sy", 16)
            t.wait_ge(sem["d_sy"], C["d_sy"].v)
            for dc in range(NDHI):
                kb.pe_t(xpwT[:, dc * 192: dc * 192 + 128], tm_tile[:, bass.ts(dc, 128)])
            kb.wait(sy, "s_sc")
            kb.inc("sy", sy.dma_start(out=tm_tile[0:64, :], in_=xpw_in[128:192, :]), "d_sy", 16)
            t.wait_ge(sem["d_sy"], C["d_sy"].v)
            for dc in range(NDHI):
                kb.pe_t(xpwT[:, dc * 192 + 128: dc * 192 + 192], tm_tile[0:64, bass.ts(dc, 128)])
            kb.wait(sy, "s_sc")
            kb.inc("sy", sy.dma_start(
                out=tm_tile[:, 0:512].rearrange("p (h r) -> p h r", h=NDHI),
                in_=dtpw_in.rearrange("(h p) r -> p h r", p=128)), "d_sy", 16)
            t.wait_ge(sem["d_sy"], C["d_sy"].v)
            for dc in range(NDHI):
                kb.pe_t(dtpwT[:, bass.ts(dc, 128)], tm_tile[:, bass.ts(dc, 64)])
            for ohi in range(NDHI):
                kb.wait(sy, "s_sc")
                kb.inc("sy", sy.dma_start(out=tm_tile, in_=opw_in[bass.ts(ohi, 128), :]), "d_sy", 16)
                t.wait_ge(sem["d_sy"], C["d_sy"].v)
                for dc in range(NDHI):
                    kb.pe_t(opwT[:, dc * DIM + ohi * 128: dc * DIM + ohi * 128 + 128],
                            tm_tile[:, bass.ts(dc, 128)])
            nc.all_engine_barrier()

        def load_x_half(half):
            for blk in range(8):
                tok0 = half * HALF + blk * 128
                kb.wait(sy, "s_sc")
                kb.inc("sy", sy.dma_start(out=tm_tile, in_=x_in[tok0:tok0 + 128, :]), "d_sy", 16)
                t.wait_ge(sem["d_sy"], C["d_sy"].v)
                for dc in range(NDHI):
                    kb.pe_t(xs[:, dc * TOK + tok0: dc * TOK + tok0 + 128],
                            tm_tile[:, bass.ts(dc, 128)])
            nc.all_engine_barrier()

        def layernorm_q(half, q, pst):
            tok0 = half * HALF + q * QW
            xq = xs3[:, :, tok0:tok0 + QW]
            x3 = xln32.rearrange("p (h k) -> p h k", h=NDHI)
            kb.wait(v, "s_sc")
            kb.wait(v, "s_pe")
            kb.inc("v", v.tensor_tensor(x3, xq, xq, ALU.mult), "s_ve")
            kb.wait(t, "s_ve")
            for dc in range(NDHI):
                mm = t.matmul(pst[0:1, 0:QW], onescol_f[:], xq[:, dc, :],
                              start=(dc == 0), stop=(dc == NDHI - 1))
            kb.inc("t", mm, "s_pe")
            for dc in range(NDHI):
                mm = t.matmul(pst[0:1, QW:2 * QW], onescol_f[:], xln32[:, bass.ts(dc, QW)],
                              start=(dc == 0), stop=(dc == NDHI - 1))
            kb.inc("t", mm, "s_pe")
            kb.wait(s, "s_pe")
            kb.wait(s, "s_ve")
            kb.inc("s", s.mul(stat[0:1, 0:QW], pst[0:1, 0:QW], 1.0 / DIM), "s_sc")
            kb.inc("s", s.mul(stat[0:1, QW:2 * QW], pst[0:1, QW:2 * QW], 1.0 / DIM), "s_sc")
            kb.wait(v, "s_sc")
            kb.inc("v", v.tensor_tensor(stat[0:1, 2 * QW:3 * QW], stat[0:1, 0:QW],
                                        stat[0:1, 0:QW], ALU.mult), "s_ve")
            kb.wait(v, "s_ve")
            kb.inc("v", v.tensor_tensor(stat[0:1, QW:2 * QW], stat[0:1, QW:2 * QW],
                                        stat[0:1, 2 * QW:3 * QW], ALU.subtract), "s_ve")
            kb.wait(s, "s_ve")
            kb.inc("s", s.activation(stat[0:1, QW:2 * QW], stat[0:1, QW:2 * QW],
                                     AF.Ln, bias=EPS_LN), "s_sc")
            kb.wait(s, "s_sc")
            kb.inc("s", s.activation(stat[0:1, QW:2 * QW], stat[0:1, QW:2 * QW],
                                     AF.Exp, scale=-0.5), "s_sc")
            kb.wait(t, "s_sc")
            kb.bc_row(mean_bc[:], stat[0:1, 0:QW])
            kb.bc_row(rstd_bc[:], stat[0:1, QW:2 * QW])
            mb = mean_bc.unsqueeze(1).broadcast_to([128, NDHI, QW])
            rb = rstd_bc.unsqueeze(1).broadcast_to([128, NDHI, QW])
            kb.wait(v, "s_sc")
            kb.inc("v", v.tensor_tensor(x3, xq, mb, ALU.subtract), "s_ve")
            kb.wait(v, "s_ve")
            kb.inc("v", v.tensor_tensor(x3, x3, rb, ALU.mult), "s_ve")
            xlb = xlnb[:].rearrange("p (h k) -> p h k", h=NDHI)[:, :, q * QW:(q + 1) * QW]
            kb.wait(v, "s_ve")
            kb.inc("v", v.tensor_copy(xlb, x3), "s_ve")

        def routing_half(half, pst, plg, tag="A"):
            tl = 8
            for q in range(NQ):
                layernorm_q(half, q, pst)
                kb.wait(t, "s_ve")
                kb.wait(t, "s_sc")
                for dc in range(NDHI):
                    mm = t.matmul(plg[0:64, 0:QW], rwT[:, bass.ts(dc, 64)],
                                  xln32[:, bass.ts(dc, QW)], start=(dc == 0), stop=(dc == NDHI - 1))
                kb.inc("t", mm, "s_pe")
                kb.wait(s, "s_pe")
                kb.inc("s", s.activation(rtmp2[0:64, 0:QW], plg[0:64, 0:QW], AF.Sigmoid), "s_sc")
                kb.wait(t, "s_sc")
                for b_ in range(4):
                    kb.pe_t(sc_tm[:, (q * 4 + b_) * 64:(q * 4 + b_) * 64 + 64],
                            rtmp2[0:64, bass.ts(b_, 128)])
            s3 = sc_tm.rearrange("p (a e) -> p a e", a=tl)
            f3 = sfc.rearrange("p (a e) -> p a e", a=tl)
            f4 = sfc.rearrange("p (a g e) -> p a g e", a=tl, g=8)
            r4 = rtmp.rearrange("p (a g e) -> p a g e", a=tl, g=8)
            r42 = rtmp2.rearrange("p (a g e) -> p a g e", a=tl, g=8)
            m3 = rtmp.rearrange("p (a e) -> p a e", a=tl)
            n3 = rnk.rearrange("p (a e) -> p a e", a=tl)
            e3 = rtmp2.rearrange("p (a e) -> p a e", a=tl)
            g1 = gsml.rearrange("p (a k) -> p a k", a=tl)
            g2_ = gsml2.rearrange("p (a k) -> p a k", a=tl)
            g3 = gsml3.rearrange("p (a k) -> p a k", a=tl)
            g4 = gsml4.rearrange("p (a k) -> p a k", a=tl)
            cbb = cb_bc[:].unsqueeze(1).broadcast_to([128, tl, 64])

            def V(op):
                kb.wait(v, "s_ve")
                kb.wait(v, "s_sc")
                kb.inc("v", op(), "s_ve")
            V(lambda: v.tensor_tensor(f3, s3, cbb, ALU.add))
            V(lambda: v.tensor_reduce(g1.unsqueeze(3), f4, AX.X, ALU.max))
            m1b = g1.unsqueeze(3).broadcast_to([128, tl, 8, 8])
            V(lambda: v.tensor_tensor(r4, f4, m1b, ALU.is_equal))
            V(lambda: v.tensor_reduce(g2_.unsqueeze(3), r4, AX.X, ALU.add))
            V(lambda: v.tensor_scalar(r42, r4, BIGNEG, 0.0, ALU.mult, ALU.add))
            V(lambda: v.tensor_tensor(r42, r42, f4, ALU.add))
            V(lambda: v.tensor_reduce(g3.unsqueeze(3), r42, AX.X, ALU.max))
            V(lambda: v.tensor_scalar(g4, g2_, 2.0, 0.0, ALU.is_ge, ALU.add))
            V(lambda: v.tensor_tensor(g2_, g1, g3, ALU.subtract))
            V(lambda: v.tensor_tensor(g2_, g2_, g4, ALU.mult))
            V(lambda: v.tensor_tensor(g3, g3, g2_, ALU.add))
            V(lambda: v.tensor_tensor(g1, g1, g3, ALU.add))
            V(lambda: v.memset(gsml4[:], 0.0))
            for j in range(8):
                gj = g1[:, :, j:j + 1].broadcast_to([128, tl, 8])
                ltj = ltg_bc[:, j * 8:(j + 1) * 8].unsqueeze(1).broadcast_to([128, tl, 8])
                V(lambda: v.tensor_tensor(g2_, gj, g1, ALU.is_gt))
                V(lambda: v.tensor_tensor(g4, g4, g2_, ALU.add))
                V(lambda: v.tensor_tensor(g2_, gj, g1, ALU.is_equal))
                V(lambda: v.tensor_tensor(g2_, g2_, ltj, ALU.mult))
                V(lambda: v.tensor_tensor(g4, g4, g2_, ALU.add))
            V(lambda: v.tensor_scalar(g4, g4, 4.0, 0.0, ALU.is_lt, ALU.add))
            gselb = g4.unsqueeze(3).broadcast_to([128, tl, 8, 8])
            V(lambda: v.tensor_tensor(r4, f4, gselb, ALU.mult))
            V(lambda: v.memset(rnk[:], 0.0))
            ve_reg = v.alloc_register(f"ve_rank{tag}{half}")
            v.reg_mov(ve_reg, C["s_ve"].v)
            with fori(nc, 0, 64) as j:
                mjv = m3[:, :, bass.ts(j, 1)].broadcast_to([128, tl, 64])
                ltjv = lt64_bc[:, bass.ts(j, 64)].unsqueeze(1).broadcast_to([128, tl, 64])
                for opi, opf in enumerate([
                        lambda: v.tensor_tensor(e3, mjv, m3, ALU.is_gt),
                        lambda: v.tensor_tensor(n3, n3, e3, ALU.add),
                        lambda: v.tensor_tensor(e3, mjv, m3, ALU.is_equal),
                        lambda: v.tensor_tensor(e3, e3, ltjv, ALU.mult),
                        lambda: v.tensor_tensor(n3, n3, e3, ALU.add)]):
                    v.wait_ge(sem["s_ve"], ve_reg)
                    v.reg_add(ve_reg, ve_reg, 1)
                    opf().then_inc(sem["s_ve"], 1)
            C["s_ve"].inc(64 * 5)
            v.free_register(ve_reg)
            V(lambda: v.tensor_scalar(rnk[:], rnk[:], 8.0, 0.0, ALU.is_lt, ALU.add))
            V(lambda: v.tensor_tensor(rtmp2[:], sc_tm[:], rnk[:], ALU.mult))
            tw3 = rtmp2.rearrange("p (a e) -> p a e", a=tl)
            V(lambda: v.tensor_reduce(gsml[:, 0:8].unsqueeze(2), tw3, AX.X, ALU.add))
            V(lambda: v.tensor_scalar(gsml[:, 0:8], gsml[:, 0:8], 1e-20, 0.0, ALU.add, ALU.add))
            V(lambda: v.reciprocal(gsml[:, 0:8], gsml[:, 0:8]))
            recb = gsml[:, 0:8].unsqueeze(2).broadcast_to([128, tl, 64])
            V(lambda: v.tensor_tensor(tw3, tw3, recb, ALU.mult))
            V(lambda: v.tensor_scalar(rtmp2[:], rtmp2[:], 1.25, 0.0, ALU.mult, ALU.add))
            kb.wait(t, "s_ve")
            kb.wait(s, "d_st")
            for a in range(tl):
                kb.pe_t(c_fmb[0:64, bass.ts(a, 128)], rtmp2[:, bass.ts(a, 64)])
            kb.wait(v, "s_sc")
            kb.wait(v, "d_st")
            kb.inc("v", v.memset(c_fmb[64:68, :], 0.5), "s_ve")
            kb.wait(sy, "s_ve")
            kb.wait(sy, "s_sc")
            kb.inc("sy", sy.dma_start(out=c_stage[:, half * HALF:(half + 1) * HALF],
                                      in_=c_fmb[:]), "d_st", 16)
            nc.all_engine_barrier()

        def ffn_half(half, pd, pc, tag="A"):
            pu = [kb.ptr[0], kb.ptr[1], kb.pbc[0], kb.pbc[1]]
            PE0, SC0, VE0 = C["s_pe"].v, C["s_sc"].v, C["s_ve"].v
            DWU0, DWD0, DC0 = C["d_wu"].v, C["d_wd"].v, C["d_c"].v
            h0 = half * HALF

            def LC(eng, name, init):
                r = eng.alloc_register(f"{name}_{tag}{half}")
                eng.reg_mov(r, init)
                return r

            def wb(eng, r, semname, bump):
                eng.wait_ge(sem[semname], r)
                if bump:
                    eng.reg_add(r, r, bump)

            rg_wu = LC(g, "rg_wu", PE0 - 8)
            rg_wd = LC(g, "rg_wd", PE0)
            rg_cr = LC(g, "rg_cr", PE0 - 24)
            rt_dc = LC(t, "rt_dc", DC0 + 16)
            rt_dwu = LC(t, "rt_dwu", DWU0 + 16)
            rt_dwd = LC(t, "rt_dwd", DWD0 + 16)
            rt_pc = LC(t, "rt_pc", SC0 - 1)
            rt_pu = LC(t, "rt_pu", SC0 + 6)
            rt_g2 = LC(t, "rt_g2", VE0 + 64)
            rt_pd = LC(t, "rt_pd", VE0 + 66)
            rt_upw = LC(t, "rt_upw", VE0)
            rs_bc = LC(s, "rs_bc", PE0 + 1)
            rs_up = LC(s, "rs_up", PE0 + 9)
            rs_rbf = LC(s, "rs_rbf", VE0 - 4)
            rs_cbw = LC(s, "rs_cbw", VE0 - 16)
            rv_sc = LC(v, "rv_sc", SC0 + 10)
            rv_dn = LC(v, "rv_dn", PE0 + 25)
            rv_self = LC(v, "rv_self", VE0)

            g.wait_ge(sem["d_st"], C["d_st"].v)
            with fori(nc, 0, NQUAD) as qd:
                up_q = all_up[bass.ts(qd, 4)]
                dn_q = all_dn[bass.ts(qd, 4)]
                cs_q = c_stage[bass.ts(qd, 4)]
                wb(g, rg_cr, "s_pe", 32)
                g.dma_start(
                    out=crow_s.rearrange("o (e k) -> o e k", e=4),
                    in_=cs_q[:, h0:h0 + HALF].unsqueeze(0)
                ).then_inc(sem["d_c"], 16)
                wb(g, rg_wu, "s_pe", 32)
                g.dma_start(
                    out=wu_s[:].rearrange("p (e c i) -> p e c i", e=4, c=NDHI),
                    in_=up_q.rearrange("e (c p) i -> p e c i", p=128)
                ).then_inc(sem["d_wu"], 16)
                wb(g, rg_wd, "s_pe", 32)
                g.dma_start(
                    out=wd_s[:].rearrange("p (e c i) -> p e c i", e=4, c=4),
                    in_=dn_q.rearrange("e (c p) i -> p e c i", p=128)
                ).then_inc(sem["d_wd"], 16)

                wb(t, rt_dc, "d_c", 16)
                wb(s, rs_cbw, "s_ve", 80)
                for j in range(8):
                    eu, q = j // 2, j % 2
                    wb(t, rt_pc, "s_sc", 1)
                    t.matmul(pc[j % 2][:], ones_b[:],
                             crow_s[0:1, eu * HALF + q * QW: eu * HALF + (q + 1) * QW],
                             start=True, stop=True).then_inc(sem["s_pe"], 1)
                    wb(s, rs_bc, "s_pe", 1)
                    s.copy(cbc_s[:, eu * HALF + q * QW: eu * HALF + (q + 1) * QW],
                           pc[j % 2][:]).then_inc(sem["s_sc"], 1)
                wb(t, rt_dwu, "d_wu", 16)
                for eu in range(4):
                    for ic in range(4):
                        u = 4 * eu + ic
                        icp = ic % 2
                        if eu == 0 and ic == 1:
                            wb(t, rt_upw, "s_ve", 80)
                        wb(t, rt_pu, "s_sc", 2)
                        for dc in range(NDHI):
                            for q in range(2):
                                mm = t.matmul(
                                    pu[2 * icp + q][:],
                                    wu_s[:, eu * (NDHI * 512) + dc * 512 + ic * 128:
                                         eu * (NDHI * 512) + dc * 512 + ic * 128 + 128],
                                    xlnb[:, dc * HALF + q * QW: dc * HALF + (q + 1) * QW],
                                    start=(dc == 0), stop=(dc == NDHI - 1))
                        mm.then_inc(sem["s_pe"], 1)
                        wb(s, rs_up, "s_pe", 1)
                        wb(s, rs_rbf, "s_ve", 4)
                        for q in range(2):
                            s.activation(r_bf[:, (2 * icp + q) * QW:(2 * icp + q + 1) * QW],
                                         pu[2 * icp + q][:], AF.Relu
                                         ).then_inc(sem["s_sc"], 1)
                        wb(v, rv_sc, "s_sc", 2)
                        wb(v, rv_self, "s_ve", 0)
                        for q in range(2):
                            rsl = r_bf[:, (2 * icp + q) * QW:(2 * icp + q + 1) * QW]
                            v.tensor_tensor(rc_bf[:, bass.ts(q, QW)], rsl,
                                            cbc_s[:, eu * HALF + q * QW: eu * HALF + (q + 1) * QW],
                                            ALU.mult).then_inc(sem["s_ve"], 1)
                            v.reg_add(rv_self, rv_self, 1)
                            v.wait_ge(sem["s_ve"], rv_self)
                            v.tensor_tensor(
                                g2_s[:, eu * (4 * HALF) + ic * HALF + q * QW:
                                     eu * (4 * HALF) + ic * HALF + (q + 1) * QW],
                                rc_bf[:, bass.ts(q, QW)], rsl, ALU.mult).then_inc(sem["s_ve"], 1)
                            v.reg_add(rv_self, rv_self, 1)
                wb(t, rt_g2, "s_ve", 80)
                wb(t, rt_dwd, "d_wd", 16)
                for dc in range(NDHI):
                    dnb = [pd[0], pd[1]] if dc % 2 == 0 else [pu[2], pu[3]]
                    if dc >= 2:
                        wb(t, rt_pd, "s_ve", 2)
                    for eu in range(4):
                        for ic in range(4):
                            first = (eu == 0 and ic == 0)
                            last = (eu == 3 and ic == 3)
                            for q in range(2):
                                mm = t.matmul(
                                    dnb[q][:],
                                    wd_s[:, eu * (4 * DIM) + ic * DIM + dc * 128:
                                         eu * (4 * DIM) + ic * DIM + dc * 128 + 128],
                                    g2_s[:, eu * (4 * HALF) + ic * HALF + q * QW:
                                         eu * (4 * HALF) + ic * HALF + (q + 1) * QW],
                                    start=first, stop=last)
                    mm.then_inc(sem["s_pe"], 1)
                    wb(v, rv_dn, "s_pe", 1)
                    wb(v, rv_self, "s_ve", 0)
                    for q in range(2):
                        v.tensor_tensor(xs3[:, dc, h0 + q * QW: h0 + (q + 1) * QW],
                                        xs3[:, dc, h0 + q * QW: h0 + (q + 1) * QW],
                                        dnb[q][:], ALU.add).then_inc(sem["s_ve"], 1)
                        v.reg_add(rv_self, rv_self, 1)
                t.reg_add(rt_pd, rt_pd, 68)
                t.reg_add(rt_pc, rt_pc, 32)
                t.reg_add(rt_pu, rt_pu, 8)
                s.reg_add(rs_bc, rs_bc, 24)
                s.reg_add(rs_up, rs_up, 16)
                s.reg_add(rs_rbf, rs_rbf, 16)
                v.reg_add(rv_sc, rv_sc, 8)
                v.reg_add(rv_dn, rv_dn, 24)
            for eng_, r_ in [(g, rg_wu), (g, rg_wd), (g, rg_cr), (t, rt_dc), (t, rt_dwu),
                             (t, rt_dwd), (t, rt_pc), (t, rt_pu), (t, rt_g2), (t, rt_pd),
                             (t, rt_upw),
                             (s, rs_bc), (s, rs_up), (s, rs_rbf), (s, rs_cbw),
                             (v, rv_sc), (v, rv_dn), (v, rv_self)]:
                eng_.free_register(r_)
            C["s_pe"].inc(32 * NQUAD)
            C["s_sc"].inc(40 * NQUAD)
            C["s_ve"].inc(80 * NQUAD)
            C["d_wu"].inc(16 * NQUAD)
            C["d_wd"].inc(16 * NQUAD)
            C["d_c"].inc(16 * NQUAD)
            nc.all_engine_barrier()


        def flush_and_zero():
            for smn in ["d_sy", "d_g", "d_wu", "d_wd", "d_c", "d_st", "d_out"]:
                g.wait_ge(sem[smn], C[smn].v)
            nc.all_engine_barrier()
            for eng_, smn in [(t, "s_pe"), (s, "s_sc"), (v, "s_ve"), (g, "s_gp"),
                              (sy, "d_sy"), (sy, "d_st"), (sy, "d_out")]:
                eng_.sem_clear(sem[smn])
            nc.all_engine_barrier()
            for smn in C:
                if smn not in ("d_g", "d_wu", "d_wd", "d_c"):
                    C[smn].v = 0

        def mamba_phase():
            build_opwT()
            z_q = wu_s[:, NDHI * DIM:].bitcast(F32)
            sp1 = z_q[:, 2048:2560]
            cbcx = scratch1[:, 0:2048].bitcast(BF16)
            brow = scratch1[0:1, 2048:4096].bitcast(BF16)
            wq = scratch1[:, 4096:6144].bitcast(BF16)
            dtq = scratch1[:, 6144:8192].bitcast(BF16)
            xnqb = wu_s[:, 8192:12288]
            zDq = scratch2[:, 0:2048].bitcast(BF16)
            yq = scratch2[:, 0:2048].bitcast(BF16)
            t1b = wd_s[:, 2560:6656]
            dab = wd_s[:, 6656:10752]
            ub = wd_s[:, 10752:14848]
            Xb = [t1b, wu_s[:, 8192:12288]]
            Db = [dab, wu_s[:, 12288:16384]]
            dlo_b = wd_s[0:64, 14848:15360]
            bfm = wd_s[:, 15360:16384].bitcast(F32)[0:64, :]
            cfm = scratch2[0:64, 2048:2560]
            hb = xlnb[:, 0:4096]
            bbc = xlnb[:, 4096:8192]
            with nc.psum_tensor("pmA", [128, 512], F32) as pmA, \
                 nc.psum_tensor("pmB", [64, 512], F32) as pmB, \
                 nc.psum_tensor("pmst", [1, 2 * QW], F32) as pmst:
                for qt in range(4):
                    xq = xs3[:, :, qt * QW:(qt + 1) * QW]
                    z3 = z_q.rearrange("p (h k) -> p h k", h=NDHI)
                    zsq = scratch2[:, 0:2048]
                    kb.wait(v, "s_sc")
                    kb.wait(v, "s_pe")
                    kb.wait(v, "s_ve")
                    kb.inc("v", v.tensor_copy(z3, xq), "s_ve")
                    for hhalf in range(2):
                        zs = z_q[:, bass.ts(hhalf, 2048)]
                        kb.wait(v, "s_ve")
                        kb.wait(v, "s_pe")
                        kb.inc("v", v.tensor_tensor(zsq, zs, zs, ALU.mult), "s_ve")
                        kb.wait(t, "s_ve")
                        for dc in range(4):
                            mm = t.matmul(pmst[0:1, QW:2 * QW], onescol_f[:],
                                          zsq[:, bass.ts(dc, QW)],
                                          start=(hhalf == 0 and dc == 0),
                                          stop=(hhalf == 1 and dc == 3))
                        kb.inc("t", mm, "s_pe")
                    kb.wait(t, "s_ve")
                    for dc in range(NDHI):
                        mm = t.matmul(pmst[0:1, 0:QW], onescol_f[:], z_q[:, bass.ts(dc, QW)],
                                      start=(dc == 0), stop=(dc == NDHI - 1))
                    kb.inc("t", mm, "s_pe")
                    kb.wait(s, "s_pe")
                    kb.inc("s", s.mul(stat[0:1, 0:QW], pmst[0:1, 0:QW], 1.0 / DIM), "s_sc")
                    kb.inc("s", s.mul(stat[0:1, QW:2 * QW], pmst[0:1, QW:2 * QW], 1.0 / DIM), "s_sc")
                    kb.wait(v, "s_sc")
                    kb.inc("v", v.tensor_tensor(stat[0:1, 2 * QW:3 * QW], stat[0:1, 0:QW],
                                                stat[0:1, 0:QW], ALU.mult), "s_ve")
                    kb.wait(v, "s_ve")
                    kb.inc("v", v.tensor_tensor(stat[0:1, QW:2 * QW], stat[0:1, QW:2 * QW],
                                                stat[0:1, 2 * QW:3 * QW], ALU.subtract), "s_ve")
                    kb.wait(s, "s_ve")
                    kb.inc("s", s.activation(stat[0:1, QW:2 * QW], stat[0:1, QW:2 * QW],
                                             AF.Ln, bias=EPS_LN), "s_sc")
                    kb.wait(s, "s_sc")
                    kb.inc("s", s.activation(stat[0:1, QW:2 * QW], stat[0:1, QW:2 * QW],
                                             AF.Exp, scale=-0.5), "s_sc")
                    kb.wait(t, "s_sc")
                    kb.bc_row(mean_bc[:], stat[0:1, 0:QW])
                    kb.bc_row(rstd_bc[:], stat[0:1, QW:2 * QW])
                    mb = mean_bc[:].unsqueeze(1).broadcast_to([128, NDHI, QW])
                    rb = rstd_bc[:].unsqueeze(1).broadcast_to([128, NDHI, QW])
                    kb.wait(v, "s_sc")
                    kb.inc("v", v.tensor_tensor(z3, z3, mb, ALU.subtract), "s_ve")
                    kb.wait(v, "s_ve")
                    kb.inc("v", v.tensor_tensor(z3, z3, rb, ALU.mult), "s_ve")
                    for hhalf in range(2):
                        zs = z_q[:, bass.ts(hhalf, 2048)]
                        kb.wait(v, "s_ve")
                        kb.wait(v, "s_pe")
                        kb.inc("v", v.tensor_tensor(zsq, zs, zs, ALU.mult), "s_ve")
                        kb.wait(t, "s_ve")
                        for dc in range(4):
                            mm = t.matmul(pmst[0:1, 0:QW], onescol_f[:], zsq[:, bass.ts(dc, QW)],
                                          start=(hhalf == 0 and dc == 0),
                                          stop=(hhalf == 1 and dc == 3))
                        kb.inc("t", mm, "s_pe")
                    kb.wait(s, "s_pe")
                    kb.inc("s", s.mul(stat[0:1, 0:QW], pmst[0:1, 0:QW], 1.0 / DIM), "s_sc")
                    kb.wait(s, "s_sc")
                    kb.inc("s", s.activation(stat[0:1, 0:QW], stat[0:1, 0:QW],
                                             AF.Ln, bias=EPS_RMS), "s_sc")
                    kb.wait(s, "s_sc")
                    kb.inc("s", s.activation(stat[0:1, 0:QW], stat[0:1, 0:QW],
                                             AF.Exp, scale=-0.5), "s_sc")
                    kb.wait(t, "s_sc")
                    kb.bc_row(rstd_bc[:], stat[0:1, 0:QW])
                    dvb = dvec_fm[:].unsqueeze(2).broadcast_to([128, NDHI, QW])
                    zD3 = zDq.rearrange("p (h k) -> p h k", h=NDHI)
                    kb.wait(v, "s_sc")
                    kb.inc("v", v.tensor_tensor(zD3, z3, dvb, ALU.mult), "s_ve")
                    kb.wait(sy, "s_ve")
                    kb.inc("sy", sy.dma_start(out=zd_stage, in_=zDq), "d_st", 16)
                    rb2 = rstd_bc[:].unsqueeze(1).broadcast_to([128, NDHI, QW])
                    kb.wait(v, "s_ve")
                    kb.inc("v", v.tensor_tensor(z3, z3, rb2, ALU.mult), "s_ve")
                    nwb = nw_fm[:].unsqueeze(2).broadcast_to([128, NDHI, QW])
                    xnb3 = xnqb.rearrange("p (h k) -> p h k", h=NDHI)
                    kb.wait(v, "s_ve")
                    kb.inc("v", v.tensor_tensor(xnb3, z3, nwb, ALU.mult), "s_ve")
                    kb.wait(t, "s_ve")
                    kb.wait(t, "s_sc")
                    for dc in range(NDHI):
                        mm = t.matmul(pmA[:], xpwT[:, dc * 192: dc * 192 + 128],
                                      xnqb[:, bass.ts(dc, QW)],
                                      start=(dc == 0), stop=(dc == NDHI - 1))
                    kb.inc("t", mm, "s_pe")
                    kb.wait(s, "s_pe")
                    kb.wait(s, "d_st")
                    kb.inc("s", s.copy(dlo_b, pmA[0:64, :]), "s_sc")
                    kb.inc("s", s.copy(bfm, pmA[64:128, :]), "s_sc")
                    kb.wait(t, "s_sc")
                    for dc in range(NDHI):
                        mm = t.matmul(pmB[:], xpwT[:, dc * 192 + 128: dc * 192 + 192],
                                      xnqb[:, bass.ts(dc, QW)],
                                      start=(dc == 0), stop=(dc == NDHI - 1))
                    kb.inc("t", mm, "s_pe")
                    kb.wait(s, "s_pe")
                    kb.inc("s", s.copy(cfm, pmB[:]), "s_sc")
                    kb.wait(sy, "s_sc")
                    kb.wait(sy, "d_c")
                    kb.inc("sy", sy.dma_start(
                        out=b_stage.rearrange("m (n tt) -> n m tt", n=STATE),
                        in_=bfm.rearrange("n (m tt) -> n m tt", m=8)), "d_st", 16)
                    kb.inc("sy", sy.dma_start(
                        out=cc_stage.rearrange("m (n tt) -> n m tt", n=STATE),
                        in_=cfm.rearrange("n (m tt) -> n m tt", m=8)), "d_st", 16)
                    for dc in range(NDHI):
                        kb.wait(t, "s_sc")
                        mm = t.matmul(pmA[:], dtpwT[:, bass.ts(dc, 128)], dlo_b,
                                      start=True, stop=True)
                        kb.inc("t", mm, "s_pe")
                        bcol = dtb_fm[:, dc:dc + 1]
                        kb.wait(s, "s_pe")
                        kb.wait(s, "s_ve")
                        kb.inc("s", s.activation(sp1, pmA[:], AF.Abs, bias=bcol), "s_sc")
                        kb.wait(s, "s_sc")
                        kb.inc("s", s.activation(sp1, sp1, AF.Exp, scale=-1.0), "s_sc")
                        kb.wait(s, "s_sc")
                        kb.inc("s", s.activation(sp1, sp1, AF.Ln, bias=1.0), "s_sc")
                        kb.inc("s", s.activation(z_q[:, 2560:3072], pmA[:], AF.Relu, bias=bcol), "s_sc")
                        kb.wait(v, "s_sc")
                        kb.inc("v", v.tensor_tensor(dtq[:, bass.ts(dc, QW)], sp1,
                                                    z_q[:, 2560:3072], ALU.add), "s_ve")
                    kb.wait(v, "s_ve")
                    kb.inc("v", v.tensor_tensor(wq, dtq, xnqb, ALU.mult), "s_ve")
                    kb.wait(v, "d_st")
                    scan_m(pmA, brow, bbc, cbcx, dtq, wq, Xb, Db, ub, hb, yq)
                    kb.wait(sy, "s_ve")
                    kb.inc("sy", sy.dma_start(out=hb, in_=zd_stage), "d_st", 16)
                    kb.wait(v, "d_st")
                    kb.wait(v, "s_ve")
                    kb.wait(v, "s_gp")
                    kb.inc("v", v.tensor_tensor(yq, yq, hb, ALU.add), "s_ve")
                    oq = z_q
                    for oc in range(NDHI):
                        kb.wait(t, "s_ve")
                        kb.wait(t, "s_sc")
                        for dc in range(NDHI):
                            mm = t.matmul(pmA[:], opwT[:, dc * DIM + oc * 128: dc * DIM + oc * 128 + 128],
                                          yq[:, bass.ts(dc, QW)],
                                          start=(dc == 0), stop=(dc == NDHI - 1))
                        kb.inc("t", mm, "s_pe")
                        kb.wait(s, "s_pe")
                        kb.wait(s, "s_ve")
                        kb.inc("s", s.copy(oq[:, bass.ts(oc, QW)], pmA[:]), "s_sc")
                    kb.wait(v, "s_sc")
                    oq3 = oq.rearrange("p (h k) -> p h k", h=NDHI)
                    kb.inc("v", v.tensor_tensor(xq, xq, oq3, ALU.add), "s_ve")
            kb.wait(sy, "s_pe")
            kb.wait(sy, "s_ve")
            kb.wait(sy, "s_sc")
            kb.wait(sy, "d_st")
            kb.inc("sy", sy.dma_start(out=scratch2[0:1, 0:4096],
                                      in_=lt64_in.rearrange("a b -> (a b)").unsqueeze(0)),
                   "d_st", 16)
            kb.wait(t, "s_ve")
            kb.wait(t, "d_st")
            kb.wait(s, "s_ve")
            for j in range(8):
                kb.bc_row(lt64_bc[:, bass.ts(j, 512)], scratch2[0:1, bass.ts(j, 512)])
            nc.all_engine_barrier()

        def scan_m(pmA, brow, bbc, cbcx, dtq, wq, Xb, Db, ub, hb, yq):
            PE0, SC0, VE0, GP0, DC0 = (C["s_pe"].v, C["s_sc"].v, C["s_ve"].v,
                                       C["s_gp"].v, C["d_c"].v)
            pbk = [kb.ptr[0], kb.ptr[1], kb.pbc[0], kb.pbc[1]]
            kb.wait(g, "d_st")
            kb.wait(s, "d_st")
            kb.wait(v, "s_pe")
            ub3 = ub.rearrange("p (n tt) -> p n tt", n=STATE)
            bbc3 = bbc.rearrange("p (n tt) -> p n tt", n=STATE)
            axp = scratch2[:, 2048:4096].bitcast(BF16)
            axp3 = axp.rearrange("p (n tt) -> p n tt", n=STATE)
            dte = scratch2[:, 4096:6144].bitcast(BF16)
            dte3 = dte.rearrange("p (n tt) -> p n tt", n=STATE)

            def BS(m):
                return SC0 + 2 + 40 * m

            def exp_done(tau):
                return BS(tau // 8) + 16 + 3 * (tau % 8) + 1

            def copies_done(tau):
                if tau == 0:
                    return SC0 + 2
                return BS((tau - 1) // 8) + 16 + 3 * ((tau - 1) % 8) + 3

            def emit_copies(tau):
                m, dhi = tau // 8, tau % 8
                o0 = dhi * QW + m * CHUNK
                a_ap = a_fmb[:, dhi * STATE:(dhi + 1) * STATE].unsqueeze(2) \
                    .broadcast_to([128, STATE, 64])
                dt_ap = dtq[:, o0:o0 + CHUNK].unsqueeze(1).broadcast_to([128, STATE, 64])
                if tau >= 1:
                    s.wait_ge(sem["s_ve"], VE0 + t1_done[tau - 1])
                s.copy(axp3, a_ap).then_inc(sem["s_sc"], 1)
                s.copy(dte3, dt_ap).then_inc(sem["s_sc"], 1)

            t1_done, scan_done, hc_done = {}, {}, {}
            _k = 0
            t1_done[0] = _k + 1; _k += 1
            scan_done[0] = _k + 1; _k += 1
            for _tau in range(64):
                if _tau + 1 < 64:
                    t1_done[_tau + 1] = _k + 1; _k += 1
                hc_done[_tau] = _k + 1; _k += 1
                _k += 6
                if _tau + 1 < 64:
                    scan_done[_tau + 1] = _k + 1; _k += 1
            assert _k == 9 * 64
            emit_copies(0)

            for m in range(8):
                g.wait_ge(sem["s_pe"], PE0 + 16 * m)
                g.dma_start(out=brow, in_=b_stage[m].unsqueeze(0)).then_inc(sem["d_c"], 16)
                g.wait_ge(sem["s_pe"], PE0 + 16 * m + 8)
                g.dma_start(out=brow, in_=cc_stage[m].unsqueeze(0)).then_inc(sem["d_c"], 16)
                t.wait_ge(sem["d_c"], DC0 + 32 * m + 16)
                for j in range(8):
                    tgt = BS(m) + j - 3 if j >= 4 else max(BS(m) - 24, 0)
                    t.wait_ge(sem["s_sc"], tgt)
                    t.matmul(pbk[j % 4][:], ones_b[:], brow[0:1, bass.ts(j, 512)],
                             start=True, stop=True).then_inc(sem["s_pe"], 1)
                    if j == 0:
                        s.wait_ge(sem["s_gp"], max(GP0 + 16 * m - 1, 0))
                    s.wait_ge(sem["s_pe"], PE0 + 16 * m + j + 1)
                    s.copy(bbc[:, bass.ts(j, 512)], pbk[j % 4][:]).then_inc(sem["s_sc"], 1)
                t.wait_ge(sem["d_c"], DC0 + 32 * m + 32)
                for j in range(8):
                    t.wait_ge(sem["s_sc"], BS(m) + j + 5)
                    t.matmul(pbk[j % 4][:], ones_b[:], brow[0:1, bass.ts(j, 512)],
                             start=True, stop=True).then_inc(sem["s_pe"], 1)
                    if j == 0:
                        s.wait_ge(sem["s_ve"],
                                  VE0 + hc_done[8 * m - 1] if m >= 1 else max(VE0, 0))
                    s.wait_ge(sem["s_pe"], PE0 + 16 * m + 8 + j + 1)
                    s.copy(cbcx[:, bass.ts(j, 512)], pbk[j % 4][:]).then_inc(sem["s_sc"], 1)
                for dhi in range(NDHI):
                    tau = 8 * m + dhi
                    o0 = dhi * QW + m * CHUNK
                    w_ap = wq[:, o0:o0 + CHUNK].unsqueeze(1).broadcast_to([128, STATE, 64])
                    s.wait_ge(sem["s_ve"], VE0 + t1_done[tau])
                    s.activation(Db[tau % 2], Xb[tau % 2], AF.Exp).then_inc(sem["s_sc"], 1)
                    if tau + 1 < 64:
                        emit_copies(tau + 1)
                    g.wait_ge(sem["s_sc"], BS(m) + 8)
                    g.wait_ge(sem["s_ve"], VE0 + (scan_done[tau - 1] if tau >= 1 else 0))
                    g.tensor_tensor(ub3, w_ap, bbc3, ALU.mult).then_inc(sem["s_gp"], 1)
                    g.wait_ge(sem["s_sc"], exp_done(tau))
                    g.memset(Db[tau % 2].rearrange("p (n tt) -> p n tt", n=STATE)[:, :, 0:1],
                             0.0).then_inc(sem["s_gp"], 1)
            vk = [0]

            def vpre():
                v.wait_ge(sem["s_ve"], VE0 + vk[0])
                vk[0] += 1

            def emit_t1(tau):
                m, dhi = tau // 8, tau % 8
                o0 = dhi * QW + m * CHUNK
                x3 = Xb[tau % 2].rearrange("p (n tt) -> p n tt", n=STATE)
                v.wait_ge(sem["s_sc"], copies_done(tau))
                vpre()
                v.tensor_tensor(x3, axp3, dte3, ALU.mult).then_inc(sem["s_ve"], 1)

            def emit_scan(tau):
                v.wait_ge(sem["s_gp"], GP0 + 2 * tau + 2)
                v.wait_ge(sem["s_sc"], exp_done(tau))
                vpre()
                v.tensor_tensor_scan(hb, Db[tau % 2], ub, 0.0, ALU.mult, ALU.add
                                     ).then_inc(sem["s_ve"], 1)

            def emit_hc_tree(tau):
                m, dhi = tau // 8, tau % 8
                o0 = dhi * QW + m * CHUNK
                Xf = Xb[tau % 2]
                v.wait_ge(sem["s_sc"], BS(m) + 16)
                vpre()
                v.tensor_tensor(Xf, hb, cbcx, ALU.mult).then_inc(sem["s_ve"], 1)
                L = 2048
                for j in range(5):
                    vpre()
                    v.tensor_tensor(Xf[:, 0:L], Xf[:, 0:L], Xf[:, L:2 * L], ALU.add
                                    ).then_inc(sem["s_ve"], 1)
                    L //= 2
                vpre()
                v.tensor_tensor(yq[:, o0:o0 + CHUNK], Xf[:, 0:CHUNK],
                                Xf[:, CHUNK:2 * CHUNK], ALU.add).then_inc(sem["s_ve"], 1)

            emit_t1(0)
            emit_scan(0)
            for tau in range(64):
                if tau + 1 < 64:
                    emit_t1(tau + 1)
                emit_hc_tree(tau)
                if tau + 1 < 64:
                    emit_scan(tau + 1)
            assert vk[0] == 9 * 64
            C["s_pe"].inc(16 * 8)
            C["s_sc"].inc(40 * 8)
            C["s_ve"].inc(9 * 64)
            C["s_gp"].inc(2 * 64)
            C["d_c"].inc(32 * 8)

        def dump_stream(dst):
            for blk in range(16):
                tok0 = blk * 128
                kb.wait(v, "d_out")
                kb.wait(v, "s_sc")
                kb.inc("v", v.tensor_copy(
                    tm_tile.rearrange("p (h k) -> p h k", h=NDHI),
                    xs3[:, :, tok0:tok0 + 128]), "s_ve")
                kb.wait(t, "s_ve")
                for dc in range(NDHI):
                    kb.pe_t(tm_tile[:, bass.ts(dc, 128)], tm_tile[:, bass.ts(dc, 128)])
                kb.wait(sy, "s_sc")
                kb.inc("sy", sy.dma_start(out=dst[tok0:tok0 + 128, :],
                                          in_=tm_tile), "d_out", 16)
            sy.wait_ge(sem["d_out"], C["d_out"].v)
            nc.all_engine_barrier()

        setup()
        for half in range(2):
            load_x_half(half)
            with nc.psum_tensor(f"pstat{half}", [1, 2 * QW], F32) as pst, \
                 nc.psum_tensor(f"plogit{half}", [64, 512], F32) as plg:
                routing_half(half, pst, plg)
            if stage >= 3:
                with nc.psum_tensor(f"pd0_{half}", [128, 512], F32) as pd0, \
                     nc.psum_tensor(f"pd1_{half}", [128, 512], F32) as pd1, \
                     nc.psum_tensor(f"pc0_{half}", [128, 512], F32) as pc0, \
                     nc.psum_tensor(f"pc1_{half}", [128, 512], F32) as pc1:
                    ffn_half(half, [pd0, pd1], [pc0, pc1])

        if stage >= 4:
            mamba_phase()

        if stage >= 5:
            for half in range(2):
                with nc.psum_tensor(f"pstatB{half}", [1, 2 * QW], F32) as pst, \
                     nc.psum_tensor(f"plogitB{half}", [64, 512], F32) as plg:
                    routing_half(half, pst, plg, tag="B")
                with nc.psum_tensor(f"pdB0_{half}", [128, 512], F32) as pd0, \
                     nc.psum_tensor(f"pdB1_{half}", [128, 512], F32) as pd1, \
                     nc.psum_tensor(f"pcB0_{half}", [128, 512], F32) as pc0, \
                     nc.psum_tensor(f"pcB1_{half}", [128, 512], F32) as pc1:
                    ffn_half(half, [pd0, pd1], [pc0, pc1], tag="B")
            dump_stream(out)

        if stage in (3, 4):
            dump_stream(dbg)
            kb.wait(v, "d_out")
            kb.inc("v", v.memset(tm_tile, 0.0), "s_ve")
            kb.wait(sy, "s_ve")
            for blk in range(16):
                kb.inc("sy", sy.dma_start(out=out[blk * 128:(blk + 1) * 128, :],
                                          in_=tm_tile), "d_out", 16)
            kb.inc("g", g.dma_start(out=dbg_c, in_=c_stage), "d_g", 16)
            sy.wait_ge(sem["d_out"], C["d_out"].v)
            g.wait_ge(sem["d_g"], C["d_g"].v)

        if stage <= 2:
            with nc.psum_tensor("pdbg", [128, 512], F32) as pdbg:
                for blk in range(8):
                    tok0 = blk * 128
                    kb.wait(v, "d_out")
                    xl3 = xlnb[:].rearrange("p (h k) -> p h k", h=NDHI)[:, :, tok0:tok0 + 128]
                    kb.wait(v, "s_sc")
                    kb.inc("v", v.tensor_copy(
                        tm_tile.rearrange("p (h k) -> p h k", h=NDHI), xl3), "s_ve")
                    kb.wait(t, "s_ve")
                    for dc in range(NDHI):
                        kb.pe_t(tm_tile[:, bass.ts(dc, 128)], tm_tile[:, bass.ts(dc, 128)])
                    kb.wait(sy, "s_sc")
                    kb.inc("sy", sy.dma_start(out=dbg[HALF + tok0:HALF + tok0 + 128, :],
                                              in_=tm_tile), "d_out", 16)
            g.wait_ge(sem["d_st"], C["d_st"].v)
            kb.inc("g", g.dma_start(out=dbg_c, in_=c_stage), "d_g", 16)
            kb.wait(v, "d_out")
            kb.inc("v", v.memset(tm_tile, 0.0), "s_ve")
            kb.wait(sy, "s_ve")
            for blk in range(16):
                kb.inc("sy", sy.dma_start(out=out[blk * 128:(blk + 1) * 128, :],
                                          in_=tm_tile), "d_out", 16)
            kb.inc("sy", sy.dma_start(out=dbg[0:HALF, :], in_=x_in[0:HALF, :]), "d_out", 16)
            sy.wait_ge(sem["d_out"], C["d_out"].v)
            g.wait_ge(sem["d_g"], C["d_g"].v)
    return nc


K_CONST_VALS = CONST_VALS


def make_consts():
    ident = np.eye(128, dtype=np.float32)
    ltg = np.zeros((1, 64), dtype=np.float32)
    for j in range(8):
        for i in range(8):
            if j < i:
                ltg[0, j * 8 + i] = 1.0
    lt64 = np.zeros((64, 64), dtype=np.float32)
    for j in range(64):
        for i in range(64):
            if j < i:
                lt64[j, i] = 1.0
    return ident, ltg, lt64.reshape(8, 512)


def prep_inputs(inputs):
    x = np.ascontiguousarray(np.asarray(inputs["L0_attn_x"]).reshape(-1, DIM))
    up = np.asarray(inputs["L0_ffn_up_w"])
    dn = np.asarray(inputs["L0_ffn_down_w"])
    sup = np.asarray(inputs["L0_ffn_shared_up_w"])
    sdn = np.asarray(inputs["L0_ffn_shared_down_w"])
    all_up_a = np.concatenate(
        [up, np.stack([sup[:, j * 512:(j + 1) * 512] for j in range(4)])], axis=0)
    all_dn_a = np.concatenate([dn, sdn.reshape(4, 512, DIM)], axis=0)
    ident, ltg, lt64 = make_consts()
    base = dict(
        all_up=np.ascontiguousarray(all_up_a),
        all_dn=np.ascontiguousarray(all_dn_a),
        router_w=np.ascontiguousarray(np.asarray(inputs["L0_ffn_router_w"])),
        cb_in=np.ascontiguousarray(np.asarray(inputs["L0_ffn_correction_bias"]).reshape(1, NE)),
        xpw_in=np.ascontiguousarray(np.asarray(inputs["L0_attn_x_proj_w"])),
        dtpw_in=np.ascontiguousarray(np.asarray(inputs["L0_attn_dt_proj_w"])),
        dtb_in=np.ascontiguousarray(np.asarray(inputs["L0_attn_dt_proj_b"]).reshape(NDHI, 128)),
        alog_in=np.ascontiguousarray(np.asarray(inputs["L0_attn_A_log"])),
        dvec_in=np.ascontiguousarray(np.asarray(inputs["L0_attn_D"]).reshape(NDHI, 128)),
        nw_in=np.ascontiguousarray(np.asarray(inputs["L0_attn_norm_w"]).reshape(NDHI, 128)),
        opw_in=np.ascontiguousarray(np.asarray(inputs["L0_attn_out_proj_w"])),
        ident_in=ident, ltg_in=ltg, lt64_in=lt64,
        consts_in=np.tile(np.asarray(K_CONST_VALS, np.float32)[None, :], (128, 1)),
    )
    in_maps = []
    for c in range(NCORE):
        m = dict(base)
        m["x_in"] = np.ascontiguousarray(x[c * TOK:(c + 1) * TOK, :])
        in_maps.append(m)
    return in_maps


def kernel(**inputs):
    in_maps = prep_inputs(inputs)
    nc = build(stage=5)
    res = run_bass_kernel_spmd(nc, in_maps, core_ids=list(range(NCORE)))
    outs = [np.asarray(r["out"], dtype=np.float32) for r in res.results]
    return np.concatenate(outs, axis=0).reshape(4, 4096, DIM)
